# version 01
# baseline (speedup 1.0000x reference)
import numpy as np
import ml_dtypes
from contextlib import ExitStack
import concourse.bass as bass
import concourse.mybir as mybir
from concourse.bass_utils import run_bass_kernel_spmd

F32 = mybir.dt.float32
BF16 = mybir.dt.bfloat16
U8 = mybir.dt.uint8
I32 = mybir.dt.int32
AF = mybir.ActivationFunctionType
ALU = mybir.AluOpType
AX = mybir.AxisListType.X
BF = ml_dtypes.bfloat16

B, S, H = 4, 2048, 2048
NCORES = 8


def _l0_costab():
    inv = (1.0 / 10000.0 ** (np.arange(0, H, 2, dtype=np.float32) / H)).astype(np.float32)
    fr = np.arange(S, dtype=np.float32)[:, None] * inv[None, :]
    return np.cos(fr).astype(np.float32), np.sin(fr).astype(np.float32)


def _l1_costab():
    inv = (1.0 / 10000.0 ** (np.arange(0, 128, 2, dtype=np.float32) / 128)).astype(np.float32)
    fr = np.arange(S, dtype=np.float32)[:, None] * inv[None, :]
    return np.cos(fr).astype(np.float32), np.sin(fr).astype(np.float32)


def _mla_costab():
    inv = (1.0 / 10000.0 ** (np.arange(0, 64, 2, dtype=np.float32) / 64)).astype(np.float32)
    fr = np.arange(S, dtype=np.float32)[:, None] * inv[None, :]
    return np.cos(fr).astype(np.float32), np.sin(fr).astype(np.float32)


IDENT = np.eye(128, dtype=np.float32).astype(BF)

def _l0_colmap(p):
    cols = []
    for lh in range(8):
        if lh < 4:
            base = 512 * p + lh * 128
        else:
            base = 1024 + 512 * p + (lh - 4) * 128
        cols.extend(range(base, base + 128))
    return np.array(cols, dtype=np.int64)


def _ceil(a, b):
    return (a + b - 1) // b


def _tiles128(a):
    t = a.shape[0] // 128
    return np.ascontiguousarray(a.reshape(t, 128, -1).transpose(1, 0, 2).reshape(128, -1))



class Seq:
    ENGS = ("sync", "gpsimd", "vector", "scalar", "tensor")

    def __init__(self, es, nc):
        self.nc = nc
        self.sems = {n: es.enter_context(nc.semaphore("q_" + n)) for n in self.ENGS}
        self.cnt = {n: 0 for n in self.ENGS}
        self.prog = {n: [] for n in self.ENGS}

    def on(self, eng, fn, after=(), ndma=0):
        deps = []
        for tok in after:
            if tok is None:
                continue
            e2, c2 = tok
            if e2 != eng:
                deps.append((e2, c2))
        inc = 16 * ndma if ndma else 1
        self.cnt[eng] += inc
        target = self.cnt[eng]
        self.prog[eng].append((deps, fn, 0 if ndma else 1))
        return (eng, target)

    def emit(self, block):
        for eng in self.ENGS:
            steps = self.prog[eng]
            if not steps:
                continue

            def body(e, steps=steps, eng=eng):
                sem = self.sems[eng]
                for deps, fn, inc in steps:
                    for e2, c2 in deps:
                        e.wait_ge(self.sems[e2], c2)
                    if inc:
                        last = fn(e)
                        last.then_inc(sem, 1)
                    else:
                        fn(e, sem)
            getattr(block, eng)(body)

    def finish(self, block, waits):
        def body(e):
            for e2, c2 in waits:
                e.wait_ge(self.sems[e2], c2)
        block.sync(body)



def build_l0_qkv():
    nc = bass.Bass("TRN2", target_bir_lowering=False, debug=False)
    xT_d = nc.dram_tensor("xT", [H, S], F32, kind="ExternalInput").ap()
    wq_d = nc.dram_tensor("wq", [H, 1024], F32, kind="ExternalInput").ap()
    wk_d = nc.dram_tensor("wk", [H, 1024], F32, kind="ExternalInput").ap()
    wv_d = nc.dram_tensor("wv", [H, 1024], F32, kind="ExternalInput").ap()
    cos_d = nc.dram_tensor("cosT", [128, 4 * S], F32, kind="ExternalInput").ap()
    sin_d = nc.dram_tensor("sinT", [128, 4 * S], F32, kind="ExternalInput").ap()
    qT_d = nc.dram_tensor("qT", [1024, S], BF16, kind="ExternalOutput").ap()
    kT_d = nc.dram_tensor("kT", [1024, S], BF16, kind="ExternalOutput").ap()
    vt_d = nc.dram_tensor("vtok", [S, 1024], BF16, kind="ExternalOutput").ap()

    with ExitStack() as es:
        sb = lambda n, sh, dt: es.enter_context(nc.sbuf_tensor(n, sh, dt))
        xT = sb("xT_s", [128, 16 * S], BF16)
        wb = sb("wb", [128, 16 * 1024], BF16)
        cosT = sb("cosT_s", [128, 4 * S], F32)
        sinT = sb("sinT_s", [128, 4 * S], F32)
        scr = sb("scr", [128, S], F32)
        tmp1 = sb("tmp1", [128, S], F32)
        tmp2 = sb("tmp2", [128, S], F32)
        ro = sb("ro", [128, 2 * S], BF16)
        vo = sb("vo", [128, 2 * 1024], BF16)
        fscr = sb("fscr", [128, 8], F32)
        psQ = es.enter_context(nc.psum_tensor("psQ", [128, 2048], F32))
        psV = es.enter_context(nc.psum_tensor("psV", [128, 1024], F32))
        block = es.enter_context(nc.Block())
        q = Seq(es, nc)

        def ldtab(e, sem):
            e.dma_start(out=cosT[:], in_=cos_d).then_inc(sem, 16)
            e.dma_start(out=sinT[:], in_=sin_d).then_inc(sem, 16)
        tTab = q.on("sync", ldtab, ndma=2)

        def ldx(e, sem):
            for i in range(16):
                e.dma_start(out=xT[:, i * S:(i + 1) * S],
                            in_=xT_d[128 * i:128 * (i + 1), :]).then_inc(sem, 16)
        tX = q.on("gpsimd", ldx, ndma=16)

        tW = [None] * 3
        tLastUse = [None] * 3
        tEv = None
        tRo = None
        tSt = []
        prevB = None
        prevEvac = None
        prevVedone = None
        ro_store = []
        vo_store = []
        for w in range(3):
            wd = [wq_d, wk_d, wv_d][w]

            def ldw(e, sem, wd=wd):
                for i in range(16):
                    e.dma_start(out=wb[:, i * 1024:(i + 1) * 1024],
                                in_=wd[128 * i:128 * (i + 1), :]).then_inc(sem, 16)
            tW[w] = q.on("gpsimd", ldw, after=[tLastUse[w - 1] if w else None], ndma=16)

            if w < 2:
                for a in range(4):
                    def mmA(e, a=a):
                        for n in range(4):
                            for i in range(16):
                                ins = e.matmul(psQ[:, n * 512:(n + 1) * 512],
                                               wb[:, i * 1024 + a * 128:i * 1024 + (a + 1) * 128],
                                               xT[:, i * S + n * 512:i * S + (n + 1) * 512],
                                               start=(i == 0), stop=(i == 15))
                        return ins
                    tA_ = q.on("tensor", mmA, after=[tX, tW[w], prevEvac])

                    def evA(e):
                        e.tensor_copy(scr[:], psQ[:])
                        return e.tensor_copy(fscr[:, 1:2], scr[:, S - 1:S])
                    prevEvac = q.on("vector", evA, after=[tA_])

                    def mmB(e, a=a):
                        for n in range(4):
                            for i in range(16):
                                ins = e.matmul(psQ[:, n * 512:(n + 1) * 512],
                                               wb[:, i * 1024 + (a + 4) * 128:i * 1024 + (a + 5) * 128],
                                               xT[:, i * S + n * 512:i * S + (n + 1) * 512],
                                               start=(i == 0), stop=(i == 15))
                        return ins
                    tB_ = q.on("tensor", mmB, after=[prevEvac])
                    tLastUse[w] = tB_

                    def rope(e, a=a):
                        e.tensor_tensor(tmp1[:], scr[:], cosT[:, a * S:(a + 1) * S], ALU.mult)
                        e.tensor_tensor(tmp2[:], psQ[:], sinT[:, a * S:(a + 1) * S], ALU.mult)
                        e.tensor_tensor(ro[:, 0:S], tmp1[:], tmp2[:], ALU.subtract)
                        e.tensor_tensor(tmp1[:], psQ[:], cosT[:, a * S:(a + 1) * S], ALU.mult)
                        e.tensor_tensor(tmp2[:], scr[:], sinT[:, a * S:(a + 1) * S], ALU.mult)
                        e.tensor_tensor(ro[:, S:2 * S], tmp1[:], tmp2[:], ALU.add)
                        return e.tensor_copy(fscr[:, 2:3], ro[:, 2 * S - 1:2 * S])
                    prevEvac = q.on("vector", rope, after=[tB_, tTab,
                                                           ro_store[-1] if ro_store else None])

                    od = [qT_d, kT_d][w]

                    def st2(e, sem, a=a, od=od):
                        e.dma_start(out=od[a * 128:(a + 1) * 128, :], in_=ro[:, 0:S]).then_inc(sem, 16)
                        e.dma_start(out=od[(a + 4) * 128:(a + 5) * 128, :], in_=ro[:, S:2 * S]).then_inc(sem, 16)
                    ro_store.append(q.on("sync", st2, after=[prevEvac], ndma=2))
            else:
                prevV = None
                for j in range(16):
                    def mmV(e, j=j):
                        for n in range(2):
                            for i in range(16):
                                ins = e.matmul(psV[:, n * 512:(n + 1) * 512],
                                               xT[:, i * S + j * 128:i * S + j * 128 + 128],
                                               wb[:, i * 1024 + n * 512:i * 1024 + (n + 1) * 512],
                                               start=(i == 0), stop=(i == 15))
                        return ins
                    tV = q.on("tensor", mmV, after=[tW[2], prevV])
                    tLastUse[2] = tV

                    def evV(e, j=j):
                        e.tensor_copy(vo[:, (j % 2) * 1024:(j % 2 + 1) * 1024], psV[:])
                        return e.tensor_copy(fscr[:, 3:4], vo[:, (j % 2) * 1024 + 1023:(j % 2) * 1024 + 1024])
                    prevV = q.on("vector", evV, after=[tV, vo_store[-2] if len(vo_store) >= 2 else None])

                    def stV(e, sem, j=j):
                        e.dma_start(out=vt_d[128 * j:128 * (j + 1), :],
                                    in_=vo[:, (j % 2) * 1024:(j % 2 + 1) * 1024]).then_inc(sem, 16)
                    vo_store.append(q.on("sync", stV, after=[prevV], ndma=1))

        q.emit(block)
        q.finish(block, [ro_store[-1], vo_store[-1], vo_store[-2]])
    return nc



def _run(nc, in_maps):
    return run_bass_kernel_spmd(nc, in_maps, core_ids=list(range(NCORES)))


def host_l0_qkv(inputs):
    x = np.asarray(inputs["L0_attn_hidden_states"], np.float32)
    wqT = np.ascontiguousarray(np.asarray(inputs["L0_attn_Wq"], np.float32).T)
    wkT = np.ascontiguousarray(np.asarray(inputs["L0_attn_Wk"], np.float32).T)
    wvT = np.ascontiguousarray(np.asarray(inputs["L0_attn_Wv"], np.float32).T)
    cos1, sin1 = _l0_costab()
    cosT = np.ascontiguousarray(cos1.T)
    sinT = np.ascontiguousarray(sin1.T)
    in_maps = []
    for c in range(NCORES):
        b, p = c // 2, c % 2
        cm = _l0_colmap(p)
        in_maps.append({
            "xT": np.ascontiguousarray(x[b].T),
            "wq": np.ascontiguousarray(wqT[:, cm]),
            "wk": np.ascontiguousarray(wkT[:, cm]),
            "wv": np.ascontiguousarray(wvT[:, cm]),
            "cosT": _tiles128(cosT[512 * p:512 * (p + 1)]),
            "sinT": _tiles128(sinT[512 * p:512 * (p + 1)]),
        })
    return in_maps



def build_l0_attn():
    nc = bass.Bass("TRN2", target_bir_lowering=False, debug=False)
    qT_d = nc.dram_tensor("qT", [128, 8 * S], BF16, kind="ExternalInput").ap()
    kT_d = nc.dram_tensor("kT", [128, 8 * S], BF16, kind="ExternalInput").ap()
    vt_d = nc.dram_tensor("vtok", [128, 16 * 1024], BF16, kind="ExternalInput").ap()
    gate_d = nc.dram_tensor("gate", [1, 8], F32, kind="ExternalInput").ap()
    id_d = nc.dram_tensor("ident", [128, 128], BF16, kind="ExternalInput").ap()
    cT_d = nc.dram_tensor("combo", [S, 1024], BF16, kind="ExternalOutput").ap()
    SCALE = float(1.0 / np.sqrt(128.0))

    with ExitStack() as es:
        sb = lambda n, sh, dt: es.enter_context(nc.sbuf_tensor(n, sh, dt))
        qT = sb("qT_s", [128, 8 * S], BF16)
        kT = sb("kT_s", [128, 8 * S], BF16)
        vtok = sb("vt_s", [128, 16 * 1024], BF16)
        comb = sb("comb", [128, 16 * 1024], BF16)
        qs = sb("qs", [128, S], BF16)
        qf = sb("qf", [128, S], BF16)
        kf = sb("kf", [128, S], BF16)
        kft = sb("kft", [128, 16 * 128], BF16)
        tA = sb("tA", [128, S], F32)
        tB = sb("tB", [128, S], F32)
        P = sb("P", [128, S], BF16)
        PT = sb("PT", [128, 16 * 512], BF16)
        mem = sb("mem", [128, 132], BF16)
        msc = sb("msc", [128, 512], F32)
        mtmp = sb("mtmp", [128, 128], F32)
        onec = sb("onec", [128, 4], BF16)
        gbc = sb("gbc", [128, 8], F32)
        ombc = sb("ombc", [128, 8], F32)
        cols = sb("cols", [128, 8], F32)
        wcols = sb("wcols", [128, 4], F32)
        lcols = sb("lcols", [128, 4], F32)
        idt = sb("idt_s", [128, 128], BF16)
        fscr = sb("fscr", [128, 8], F32)
        psS = es.enter_context(nc.psum_tensor("psS", [128, 2048], F32))
        psT = es.enter_context(nc.psum_tensor("psT", [128, 1024], BF16))
        psM = es.enter_context(nc.psum_tensor("psM", [128, 512], F32))
        psD = es.enter_context(nc.psum_tensor("psD", [128, 512], F32))
        psA = es.enter_context(nc.psum_tensor("psA", [128, 512], F32))
        block = es.enter_context(nc.Block())
        q = Seq(es, nc)

        def ld(e, sem):
            e.dma_start(out=qT[:], in_=qT_d).then_inc(sem, 16)
            e.dma_start(out=kT[:], in_=kT_d).then_inc(sem, 16)
            e.dma_start(out=vtok[:], in_=vt_d).then_inc(sem, 16)
            e.dma_start(out=idt[:], in_=id_d).then_inc(sem, 16)
            e.dma_start(out=gbc[:], in_=gate_d.broadcast_to([128, 8])).then_inc(sem, 16)
        tLD = q.on("sync", ld, ndma=5)

        def prep_g(e):
            e.activation(gbc[:], gbc[:], AF.Sigmoid)
            return e.copy(fscr[:, 0:1], gbc[:, 7:8])
        tG = q.on("scalar", prep_g, after=[tLD])

        def prep_omg(e):
            e.tensor_scalar(ombc[:], gbc[:], -1.0, 1.0, ALU.mult, ALU.add)
            e.memset(onec[:], 1.0)
            return e.tensor_copy(fscr[:, 1:2], ombc[:, 7:8])
        tOMG = q.on("vector", prep_omg, after=[tG])

        tC = [None] * 128
        tPTev = None
        tMemEv = None
        tAttEv = None
        tDev = None
        for lh in range(8):
            hOFF = lh * S

            def qsv(e, hOFF=hOFF):
                e.tensor_scalar_mul(qs[:], qT[:, hOFF:hOFF + S], SCALE)
                return e.tensor_copy(fscr[:, 2:3], qs[:, S - 1:S])
            tQS = q.on("vector", qsv, after=[tLD])

            def kf1(e, hOFF=hOFF):
                e.tensor_single_scalar(tA[:], kT[:, hOFF:hOFF + S], 0.0, ALU.min)
                return e.tensor_copy(fscr[:, 2:3], tA[:, S - 1:S])
            t1 = q.on("vector", kf1, after=[tLD])

            def kf2(e):
                e.activation(tB[:], tA[:], AF.Exp)
                return e.copy(fscr[:, 3:4], tB[:, S - 1:S])
            t2 = q.on("scalar", kf2, after=[t1])

            def kf3(e, hOFF=hOFF):
                e.tensor_single_scalar(tA[:], kT[:, hOFF:hOFF + S], 0.0, ALU.max)
                e.tensor_tensor(kf[:], tB[:], tA[:], ALU.add)
                return e.tensor_copy(fscr[:, 2:3], kf[:, S - 1:S])
            t3 = q.on("vector", kf3, after=[t2])

            for bt in range(2):
                def trk(e, bt=bt):
                    for i in range(8):
                        ins = e.transpose(psT[:, i * 128:(i + 1) * 128],
                                          kf[:, (bt * 8 + i) * 128:(bt * 8 + i + 1) * 128], idt[:])
                    return ins
                tt = q.on("tensor", trk, after=[t3, tPTev])

                def evk(e, bt=bt):
                    for i in range(8):
                        e.copy(kft[:, (bt * 8 + i) * 128:(bt * 8 + i + 1) * 128],
                               psT[:, i * 128:(i + 1) * 128])
                    return e.copy(fscr[:, 4:5], kft[:, (bt * 8 + 7) * 128 + 127:(bt * 8 + 7) * 128 + 128])
                tPTev = q.on("scalar", evk, after=[tt])
            tTrKdone = tPTev

            def qf1(e, hOFF=hOFF):
                e.tensor_single_scalar(tA[:], qT[:, hOFF:hOFF + S], 0.0, ALU.min)
                return e.tensor_copy(fscr[:, 2:3], tA[:, S - 1:S])
            t4 = q.on("vector", qf1, after=[t3])

            def qf2(e):
                e.activation(tB[:], tA[:], AF.Exp)
                return e.copy(fscr[:, 3:4], tB[:, S - 1:S])
            t5 = q.on("scalar", qf2, after=[t4])

            def qf3(e, hOFF=hOFF):
                e.tensor_single_scalar(tA[:], qT[:, hOFF:hOFF + S], 0.0, ALU.max)
                e.tensor_tensor(qf[:], tB[:], tA[:], ALU.add)
                return e.tensor_copy(fscr[:, 2:3], qf[:, S - 1:S])
            t6 = q.on("vector", qf3, after=[t5])

            def memmm(e, lh=lh):
                for j in range(16):
                    e.matmul(psM[:, 0:128], kft[:, j * 128:(j + 1) * 128],
                             vtok[:, j * 1024 + lh * 128:j * 1024 + (lh + 1) * 128],
                             start=(j == 0), stop=(j == 15))
                for j in range(16):
                    ins = e.matmul(psM[:, 128:129], kft[:, j * 128:(j + 1) * 128],
                                   onec[:, 0:1], start=(j == 0), stop=(j == 15))
                return ins
            tMM = q.on("tensor", memmm, after=[tTrKdone, tOMG, tMemEv])

            def memev(e):
                e.copy(mem[:, 0:129], psM[:, 0:129])
                return e.copy(fscr[:, 5:6], mem[:, 128:129])
            tMemEv = q.on("scalar", memev, after=[tMM])

            for qg in range(4):
                tPh = []
                tEx = [None] * 4
                tWS = [None] * 4
                tLS = [None] * 4
                for qc in range(4):
                    jq = qg * 4 + qc

                    def smm(e, hOFF=hOFF, jq=jq):
                        for n in range(4):
                            ins = e.matmul(psS[:, n * 512:(n + 1) * 512],
                                           qs[:, jq * 128:(jq + 1) * 128],
                                           kT[:, hOFF + n * 512:hOFF + (n + 1) * 512],
                                           start=True, stop=True)
                        return ins
                    tS = q.on("tensor", smm, after=[tQS, tEx[qc - 1] if qc else tPTev, None if qc else tAttEv])

                    def smax(e):
                        e.reduce_max(cols[:, 0:1], psS[:], axis=AX, negate=True)
                        return e.tensor_copy(cols[:, 1:2], cols[:, 0:1])
                    tMx = q.on("vector", smax, after=[tS])

                    def sexp(e):
                        e.activation(P[:], psS[:], AF.Exp, bias=cols[:, 0:1], scale=1.0,
                                     accum_out=cols[:, 3:4])
                        e.copy(fscr[:, 6:7], P[:, S - 1:S])
                        return e.copy(fscr[:, 6:7], cols[:, 3:4])
                    tEx[qc] = q.on("scalar", sexp, after=[tMx])

                    def vrec(e):
                        e.reciprocal(cols[:, 4:5], cols[:, 3:4])
                        return e.tensor_copy(cols[:, 5:6], cols[:, 4:5])
                    tR = q.on("vector", vrec, after=[tEx[qc]])

                    def swc(e, lh=lh, qc=qc):
                        e.mul(wcols[:, qc:qc + 1], cols[:, 4:5], ombc[:, lh:lh + 1])
                        return e.copy(fscr[:, 7:8], wcols[:, qc:qc + 1])
                    tWS[qc] = q.on("scalar", swc, after=[tR])

                    for bt in range(2):
                        def trp(e, bt=bt):
                            for i in range(8):
                                ins = e.transpose(psT[:, i * 128:(i + 1) * 128],
                                                  P[:, (bt * 8 + i) * 128:(bt * 8 + i + 1) * 128], idt[:])
                            return ins
                        tt = q.on("tensor", trp, after=[tEx[qc], tPTev])

                        def evp(e, bt=bt, qc=qc):
                            for i in range(8):
                                j = bt * 8 + i
                                e.copy(PT[:, j * 512 + qc * 128:j * 512 + (qc + 1) * 128],
                                       psT[:, i * 128:(i + 1) * 128])
                            return e.copy(fscr[:, 4:5], PT[:, (bt * 8 + 7) * 512 + qc * 128 + 127:(bt * 8 + 7) * 512 + qc * 128 + 128])
                        tPTev = q.on("scalar", evp, after=[tt])
                        tPh.append(tPTev)

                    def dmm(e, jq=jq):
                        return e.matmul(psD[:, 0:129], qf[:, jq * 128:(jq + 1) * 128],
                                        mem[:, 0:129], start=True, stop=True)
                    tD = q.on("tensor", dmm, after=[tMemEv, t6, tDev, None if qc else tAttEv])

                    def dev(e, qc=qc):
                        e.tensor_copy(msc[:, qc * 128:(qc + 1) * 128], psD[:, 0:128])
                        e.reciprocal(cols[:, 6:7], psD[:, 128:129])
                        return e.tensor_copy(cols[:, 7:8], cols[:, 6:7])
                    tDev = q.on("vector", dev, after=[tD])

                    def slc(e, lh=lh, qc=qc):
                        e.mul(lcols[:, qc:qc + 1], cols[:, 6:7], gbc[:, lh:lh + 1])
                        return e.copy(fscr[:, 7:8], lcols[:, qc:qc + 1])
                    tLS[qc] = q.on("scalar", slc, after=[tDev])

                def amm(e, lh=lh):
                    for qc in range(4):
                        for j in range(16):
                            ins = e.matmul(psA[:, qc * 128:(qc + 1) * 128],
                                           PT[:, j * 512 + qc * 128:j * 512 + (qc + 1) * 128],
                                           vtok[:, j * 1024 + lh * 128:j * 1024 + (lh + 1) * 128],
                                           start=(j == 0), stop=(j == 15))
                    return ins
                tAm = q.on("tensor", amm, after=[tPh[-1], tAttEv])

                def combv(e, lh=lh, qg=qg):
                    for qc in range(4):
                        jq = qg * 4 + qc
                        e.tensor_scalar(mtmp[:], msc[:, qc * 128:(qc + 1) * 128],
                                        lcols[:, qc:qc + 1], None, ALU.mult)
                        e.scalar_tensor_tensor(
                            comb[:, jq * 1024 + lh * 128:jq * 1024 + (lh + 1) * 128],
                            psA[:, qc * 128:(qc + 1) * 128], wcols[:, qc:qc + 1], mtmp[:],
                            ALU.mult, ALU.add)
                    return e.tensor_copy(fscr[:, 2:3], comb[:, (qg * 4 + 3) * 1024 + lh * 128 + 127:(qg * 4 + 3) * 1024 + lh * 128 + 128])
                tAttEv = q.on("vector", combv, after=[tAm, tWS[3], tLS[3], tDev])
                for qc in range(4):
                    tC[(qg * 4 + qc) * 8 + lh] = tAttEv

        def stc(e, sem):
            for j in range(16):
                e.dma_start(out=cT_d[j * 128:(j + 1) * 128, :],
                            in_=comb[:, j * 1024:(j + 1) * 1024]).then_inc(sem, 16)
        tSt = q.on("sync", stc, after=[tC[127]], ndma=16)

        q.emit(block)
        q.finish(block, [tSt])
    return nc


def host_l0_attn(inputs, a1_results):
    g = np.asarray(inputs["L0_attn_gate"], np.float32).reshape(1, 16)
    in_maps = []
    for c in range(NCORES):
        p = c % 2
        gh = [4 * p + i for i in range(4)] + [8 + 4 * p + i for i in range(4)]
        gsel = g[:, gh]
        r = a1_results[c]
        in_maps.append({
            "qT": _tiles128(r["qT"]), "kT": _tiles128(r["kT"]), "vtok": _tiles128(r["vtok"]),
            "gate": np.ascontiguousarray(gsel),
            "ident": IDENT,
        })
    return in_maps



def ln_zigzag(q, t_xo, xo, xc, scrap, cols, fscr, c0, N, eps, apply_fn, extra_after=()):
    inv_n = 1.0 / float(N)

    def s_mean(e):
        e.mul(cols[:, c0 + 1:c0 + 2], cols[:, c0:c0 + 1], inv_n)
        return e.copy(fscr[:, 0:1], cols[:, c0 + 1:c0 + 2])
    t1 = q.on("scalar", s_mean, after=[t_xo])

    def v_center(e):
        e.tensor_scalar(xc[:], xo[:], cols[:, c0 + 1:c0 + 2], None, ALU.subtract)
        return e.tensor_copy(fscr[:, 1:2], xc[:, xc.shape[1] - 1:xc.shape[1]])
    t2 = q.on("vector", v_center, after=[t1])

    def s_sq(e):
        e.activation(scrap[:], xc[:], AF.Square, accum_out=cols[:, c0 + 2:c0 + 3])
        return e.copy(fscr[:, 0:1], cols[:, c0 + 2:c0 + 3])
    t3 = q.on("scalar", s_sq, after=[t2])

    def v_w(e):
        e.tensor_scalar(cols[:, c0 + 3:c0 + 4], cols[:, c0 + 2:c0 + 3], inv_n, eps, ALU.mult, ALU.add)
        return e.tensor_copy(fscr[:, 1:2], cols[:, c0 + 3:c0 + 4])
    t4 = q.on("vector", v_w, after=[t3])

    def s_sqrt(e):
        e.activation(cols[:, c0 + 4:c0 + 5], cols[:, c0 + 3:c0 + 4], AF.Sqrt)
        return e.copy(fscr[:, 0:1], cols[:, c0 + 4:c0 + 5])
    t5 = q.on("scalar", s_sqrt, after=[t4])

    def v_rstd(e):
        e.reciprocal(cols[:, c0 + 5:c0 + 6], cols[:, c0 + 4:c0 + 5])
        return e.tensor_copy(fscr[:, 1:2], cols[:, c0 + 5:c0 + 6])
    t6 = q.on("vector", v_rstd, after=[t5])

    return q.on("scalar", apply_fn, after=[t6] + list(extra_after))



def build_oproj_ln(resid_dtype=F32):
    nc = bass.Bass("TRN2", target_bir_lowering=False, debug=False)
    cbT_d = nc.dram_tensor("cbT", [128, 16 * 1024], BF16, kind="ExternalInput").ap()
    woT_d = nc.dram_tensor("woT", [H, H], F32, kind="ExternalInput").ap()
    x_d = nc.dram_tensor("x", [1024, H], F32, kind="ExternalInput").ap()
    x1f_d = nc.dram_tensor("x1f", [1024, H], F32, kind="ExternalOutput").ap()
    x1b_d = nc.dram_tensor("x1b", [1024, H], BF16, kind="ExternalOutput").ap()

    with ExitStack() as es:
        sb = lambda n, sh, dt: es.enter_context(nc.sbuf_tensor(n, sh, dt))
        cbT = sb("cbT_s", [128, 16 * 1024], BF16)
        woT = sb("woT_s", [128, 16 * H], BF16)
        xres = sb("xres", [128, 2 * H], F32)
        xo = sb("xo", [128, H], F32)
        xc = sb("xc", [128, H], F32)
        scrap = sb("scrap", [128, H], F32)
        x1f = sb("x1f_s", [128, 2 * H], F32)
        x1b = sb("x1b_s", [128, 2 * H], BF16)
        cols = sb("cols", [128, 8], F32)
        fscr = sb("fscr", [128, 8], F32)
        psO = es.enter_context(nc.psum_tensor("psO", [128, 2048], F32))
        block = es.enter_context(nc.Block())
        q = Seq(es, nc)

        def ld(e, sem):
            e.dma_start(out=cbT[:], in_=cbT_d).then_inc(sem, 16)
        tLD = q.on("sync", ld, ndma=1)

        def ldw(e, sem):
            for i in range(16):
                e.dma_start(out=woT[:, i * H:(i + 1) * H],
                            in_=woT_d[128 * i:128 * (i + 1), :]).then_inc(sem, 16)
        tW = q.on("gpsimd", ldw, ndma=16)

        tXO = [None] * 8
        tAPP = [None] * 8
        tStore = [None] * 8
        for jt in range(8):
            def ldx(e, sem, jt=jt):
                e.dma_start(out=xres[:, (jt % 2) * H:(jt % 2 + 1) * H],
                            in_=x_d[128 * jt:128 * (jt + 1), :]).then_inc(sem, 16)
            tX = q.on("sync", ldx, after=[tXO[jt - 2] if jt >= 2 else None], ndma=1)

            def mm(e, jt=jt):
                for n in range(4):
                    for i in range(16):
                        ins = e.matmul(psO[:, n * 512:(n + 1) * 512],
                                       cbT[:, i * 1024 + jt * 128:i * 1024 + (jt + 1) * 128],
                                       woT[:, i * H + n * 512:i * H + (n + 1) * 512],
                                       start=(i == 0), stop=(i == 15))
                return ins
            tM = q.on("tensor", mm, after=[tLD, tW, tXO[jt - 1] if jt >= 1 else None])

            def v_xo(e, jt=jt):
                e.scalar_tensor_tensor(xo[:], psO[:], 1.0, xres[:, (jt % 2) * H:(jt % 2 + 1) * H],
                                       ALU.mult, ALU.add, accum_out=cols[:, 0:1])
                return e.tensor_copy(fscr[:, 1:2], cols[:, 0:1])
            tXO[jt] = q.on("vector", v_xo, after=[tM, tX])

            def apply(e, jt=jt):
                e.activation(x1f[:, (jt % 2) * H:(jt % 2 + 1) * H], xc[:], AF.Copy,
                             scale=cols[:, 5:6])
                e.copy(x1b[:, (jt % 2) * H:(jt % 2 + 1) * H],
                       x1f[:, (jt % 2) * H:(jt % 2 + 1) * H])
                e.copy(fscr[:, 0:1], x1f[:, (jt % 2) * H + H - 1:(jt % 2) * H + H])
                return e.copy(fscr[:, 0:1], x1b[:, (jt % 2) * H + H - 1:(jt % 2) * H + H])
            tAPP[jt] = ln_zigzag(q, tXO[jt], xo, xc, scrap, cols, fscr, 0, H, 1e-5, apply,
                                 extra_after=[tStore[jt - 2] if jt >= 2 else None])

            def st(e, sem, jt=jt):
                e.dma_start(out=x1f_d[128 * jt:128 * (jt + 1), :],
                            in_=x1f[:, (jt % 2) * H:(jt % 2 + 1) * H]).then_inc(sem, 16)
                e.dma_start(out=x1b_d[128 * jt:128 * (jt + 1), :],
                            in_=x1b[:, (jt % 2) * H:(jt % 2 + 1) * H]).then_inc(sem, 16)
            tStore[jt] = q.on("sync", st, after=[tAPP[jt]], ndma=2)

        q.emit(block)
        q.finish(block, [tStore[6], tStore[7]])
    return nc


def host_oproj_ln(inputs, a2_results):
    x = np.asarray(inputs["L0_attn_hidden_states"], np.float32)
    woT = np.ascontiguousarray(np.asarray(inputs["L0_attn_Wo"], np.float32).T)
    comb = np.zeros((B, S, H), dtype=BF)
    for c in range(NCORES):
        b, p = c // 2, c % 2
        cm = _l0_colmap(p)
        comb[b][:, cm] = np.asarray(a2_results[c]["combo"])
    in_maps = []
    for c in range(NCORES):
        b, half = c // 2, c % 2
        sl = slice(half * 1024, (half + 1) * 1024)
        cbT = np.ascontiguousarray(comb[b][sl].T)
        in_maps.append({
            "cbT": _tiles128(cbT),
            "woT": woT,
            "x": np.ascontiguousarray(x[b][sl]),
        })
    return in_maps



TT_ = 8192


def _emit_decode(q, nc, tiles, src_u8, scl, dst, dst_off, after, m75=None):
    tA, tB, tBi, nb1, nb2, d1, d2, sgt = tiles
    NC = 2048

    def v_bytes(e):
        e.tensor_copy(tA[:], src_u8[:])
        e.tensor_scalar(tB[:], tA[:], 0.0625, -0.46875, ALU.mult, ALU.add)
        e.tensor_copy(tBi[:], tB[:])
        e.tensor_copy(tB[:], tBi[:])
        e.scalar_tensor_tensor(nb1[:], tB[:], -16.0, tA[:], ALU.mult, ALU.add)
        e.tensor_copy(nb2[:], tB[:])
        return e.tensor_copy(tA[:, 0:1], nb2[:, NC - 1:NC])
    tV = q.on("vector", v_bytes, after=after)

    outs = [None]
    for pi, nb in ((0, nb1), (1, nb2)):
        def s_sign(e, nb=nb):
            e.activation(sgt[:], nb[:], AF.Sign, bias=m75[:, 0:1])
            return e.copy(tA[:, 1:2], sgt[:, NC - 1:NC])
        tS = q.on("scalar", s_sign, after=[tV, outs[-1]])

        def v_dec(e, nb=nb, pi=pi):
            e.scalar_tensor_tensor(d1[:], sgt[:], -4.0, nb[:], ALU.mult, ALU.add)
            e.tensor_scalar(d2[:], d1[:], -8.0, 0.0, ALU.add, ALU.max)
            e.tensor_scalar(nb[:], d1[:], -10.0, 0.0, ALU.add, ALU.max)
            e.scalar_tensor_tensor(d2[:], nb[:], 2.0, d2[:], ALU.mult, ALU.add)
            e.tensor_tensor(d2[:], d2[:], d1[:], ALU.add)
            e.tensor_scalar(d2[:], d2[:], -4.0, None, ALU.add)
            e.tensor_tensor(d2[:], d2[:], sgt[:], ALU.mult)
            e.scalar_tensor_tensor(dst[:, dst_off + pi * NC:dst_off + (pi + 1) * NC],
                                   d2[:], -0.5, scl[:], ALU.mult, ALU.mult)
            return e.tensor_copy(tA[:, 2:3],
                                 dst[:, dst_off + (pi + 1) * NC - 1:dst_off + (pi + 1) * NC])
        outs.append(q.on("vector", v_dec, after=[tS]))
    return outs[-1]


def build_l0_ffn():
    nc = bass.Bass("TRN2", target_bir_lowering=False, debug=False)
    xT_d = nc.dram_tensor("xT", [H, TT_], BF16, kind="ExternalInput").ap()
    tg_d = nc.dram_tensor("tg", [1024, 2048], U8, kind="ExternalInput").ap()
    tu_d = nc.dram_tensor("tu", [1024, 2048], U8, kind="ExternalInput").ap()
    td_d = nc.dram_tensor("td", [1024, 2048], U8, kind="ExternalInput").ap()
    gs_d = nc.dram_tensor("gsT", [64, 2048], F32, kind="ExternalInput").ap()
    us_d = nc.dram_tensor("usT", [64, 2048], F32, kind="ExternalInput").ap()
    ds_d = nc.dram_tensor("dsT", [64, 2048], F32, kind="ExternalInput").ap()
    inter_d = nc.dram_tensor("inter", [2048, TT_], BF16, kind="Internal").ap()
    out_d = nc.dram_tensor("partial", [H, TT_], BF16, kind="ExternalOutput").ap()

    with ExitStack() as es:
        sb = lambda n, sh, dt: es.enter_context(nc.sbuf_tensor(n, sh, dt))
        wg = sb("wg", [128, 16 * 2048], BF16)
        wu = sb("wu", [128, 16 * 2048], BF16)
        u8t = sb("u8t", [128, 2048], U8)
        sclt = sb("sclt", [128, 2048], F32)
        tA = sb("tA", [128, 2048], F32)
        tB = sb("tB", [128, 2048], F32)
        tBi = sb("tBi", [128, 2048], I32)
        nb1 = sb("nb1", [128, 2048], BF16)
        nb2 = sb("nb2", [128, 2048], BF16)
        d1 = sb("d1", [128, 2048], BF16)
        d2 = sb("d2", [128, 2048], BF16)
        sgt = sb("sgt", [128, 2048], BF16)
        sq = sb("sq", [128, 512], F32)
        cc = sb("cc", [128, 512], F32)
        arg = sb("arg", [128, 512], F32)
        th = sq
        dd = cc
        e1 = arg
        ibuf = sb("ibuf", [128, 2 * 512], BF16)
        fscr = sb("fscr", [128, 8], F32)
        m75c = sb("m75c", [128, 1], F32)
        psG = es.enter_context(nc.psum_tensor("psG", [128, 1024], F32))
        psU = es.enter_context(nc.psum_tensor("psU", [128, 1024], F32))
        block = es.enter_context(nc.Block())
        q = Seq(es, nc)

        dec_tiles = (tA, tB, tBi, nb1, nb2, d1, d2, sgt)

        def mkc(e):
            e.memset(m75c[:], -7.5)
            return e.tensor_copy(fscr[:, 3:4], m75c[:])
        tM75 = q.on("vector", mkc)

        tDecDone = None
        tLdPrev = [None, None]
        for a in range(8):
            for mi, (wdst, t_d, s_d) in enumerate([(wg, tg_d, gs_d), (wu, tu_d, us_d)]):
                slot = (2 * a + mi) % 2

                def ldt(e, sem, a=a, t_d=t_d, s_d=s_d, slot=slot):
                    e.dma_start(out=u8t[:],
                                in_=t_d[128 * a:128 * (a + 1), :]).then_inc(sem, 16)
                    e.dma_start(out=sclt[:],
                                in_=s_d[8 * a:8 * (a + 1), :].unsqueeze(1).broadcast_to([8, 16, 2048])).then_inc(sem, 16)
                tL = q.on("sync", ldt, after=[tLdPrev[slot], tDecDone], ndma=2)
                tDecDone = _emit_decode(
                    q, nc, dec_tiles,
                    u8t[:],
                    sclt[:],
                    wdst, (2 * a) * 2048, after=[tL, tDecDone, tM75], m75=m75c)
                tLdPrev[slot] = tDecDone

        tXld = [None] * 16
        prev_mm = [None, None]

        def mk_xld(i):
            def f(e, sem):
                e.dma_start(out=xb[:, (i % 2) * TT_:(i % 2 + 1) * TT_],
                            in_=xT_d[128 * i:128 * (i + 1), :]).then_inc(sem, 16)
            return f
        xcb = sb("xcb", [128, 16 * 512], BF16)
        tGU = None
        tIst = [None] * 4
        for tc in range(16):
            def ldxc(e, sem, tc=tc):
                for i in range(16):
                    e.dma_start(out=xcb[:, i * 512:(i + 1) * 512],
                                in_=xT_d[128 * i:128 * (i + 1), 512 * tc:512 * (tc + 1)]).then_inc(sem, 16)
            tXc = q.on("sync", ldxc, after=[tGU], ndma=16)

            for nt in range(16):
                slot = nt % 2

                def gumm(e, nt=nt, slot=slot):
                    for i in range(16):
                        e.matmul(psG[:, slot * 512:(slot + 1) * 512],
                                 wg[:, i * 2048 + nt * 128:i * 2048 + (nt + 1) * 128],
                                 xcb[:, i * 512:(i + 1) * 512],
                                 start=(i == 0), stop=(i == 15))
                    for i in range(16):
                        ins = e.matmul(psU[:, slot * 512:(slot + 1) * 512],
                                       wu[:, i * 2048 + nt * 128:i * 2048 + (nt + 1) * 128],
                                       xcb[:, i * 512:(i + 1) * 512],
                                       start=(i == 0), stop=(i == 15))
                    return ins
                tGU = q.on("tensor", gumm,
                           after=[tXc, tDecDone, prev_mm[slot]])

                def gelu1(e, slot=slot):
                    e.activation(sq[:], psG[:, slot * 512:(slot + 1) * 512], AF.Square)
                    return e.copy(fscr[:, 0:1], sq[:, 511:512])
                tq1 = q.on("scalar", gelu1, after=[tGU])

                def gelu2(e, slot=slot):
                    e.tensor_scalar(cc[:], sq[:], 0.044715 * 0.7978845608028654,
                                    0.7978845608028654, ALU.mult, ALU.add)
                    e.tensor_tensor(arg[:], cc[:], psG[:, slot * 512:(slot + 1) * 512], ALU.mult)
                    return e.tensor_copy(fscr[:, 1:2], arg[:, 511:512])
                tq2 = q.on("vector", gelu2, after=[tq1])

                def gelu3(e):
                    e.activation(th[:], arg[:], AF.Tanh)
                    return e.copy(fscr[:, 0:1], th[:, 511:512])
                tq3 = q.on("scalar", gelu3, after=[tq2])

                def gelu4(e, slot=slot, nt=nt, tc=tc):
                    e.tensor_scalar(dd[:], th[:], 1.0, 0.5, ALU.add, ALU.mult)
                    e.tensor_tensor(e1[:], dd[:], psG[:, slot * 512:(slot + 1) * 512], ALU.mult)
                    e.tensor_tensor(ibuf[:, slot * 512:(slot + 1) * 512], e1[:],
                                    psU[:, slot * 512:(slot + 1) * 512], ALU.mult)
                    return e.tensor_copy(fscr[:, 1:2], ibuf[:, slot * 512 + 511:slot * 512 + 512])
                tq4 = q.on("vector", gelu4, after=[tq3, tIst[slot] if nt >= 2 else None])
                prev_mm[slot] = tq4

                def ist(e, sem, nt=nt, tc=tc, slot=slot):
                    e.dma_start(out=inter_d[128 * nt:128 * (nt + 1), 512 * tc:512 * (tc + 1)],
                                in_=ibuf[:, slot * 512:(slot + 1) * 512]).then_inc(sem, 16)
                tIst[slot] = q.on("sync", ist, after=[tq4], ndma=1)
        tPh1St = [tIst[0], tIst[1]]

        q2 = q
        wd = wg
        inT = xcb
        obuf = ibuf
        psO2 = psG
        tDec2 = tGU
        tLdPrev2 = [tPh1St[0], tPh1St[1]]
        for a in range(8):
            slot = a % 2

            def ldt2(e, sem, a=a, slot=slot):
                e.dma_start(out=u8t[:],
                            in_=td_d[128 * a:128 * (a + 1), :]).then_inc(sem, 16)
                e.dma_start(out=sclt[:],
                            in_=ds_d[8 * a:8 * (a + 1), :].unsqueeze(1).broadcast_to([8, 16, 2048])).then_inc(sem, 16)
            tL2 = q2.on("sync", ldt2, after=[tLdPrev2[slot], tPh1St[slot], tDec2], ndma=2)
            tDec2 = _emit_decode(q2, nc, dec_tiles,
                                 u8t[:],
                                 sclt[:],
                                 wd, (2 * a) * 2048, after=[tL2, tDec2, tGU], m75=m75c)
            tLdPrev2[slot] = tDec2

        tOev = [None, None]
        tOst = [None, None]
        tMM2 = None
        for tc in range(16):
            def ldin(e, sem, tc=tc):
                for i in range(16):
                    e.dma_start(out=inT[:, i * 512:(i + 1) * 512],
                                in_=inter_d[128 * i:128 * (i + 1), 512 * tc:512 * (tc + 1)]).then_inc(sem, 16)
            tI = q2.on("gpsimd", ldin, after=[tMM2, tPh1St[0], tPh1St[1]], ndma=16)

            for ht in range(16):
                slot = ht % 2

                def dmm2(e, ht=ht, slot=slot):
                    for ntx in range(16):
                        ins = e.matmul(psO2[:, slot * 512:(slot + 1) * 512],
                                       wd[:, ntx * 2048 + ht * 128:ntx * 2048 + ht * 128 + 128],
                                       inT[:, ntx * 512:(ntx + 1) * 512],
                                       start=(ntx == 0), stop=(ntx == 15))
                    return ins
                tMM2 = q2.on("tensor", dmm2, after=[tI, tDec2, tOev[slot]])

                def oev(e, slot=slot):
                    e.tensor_copy(obuf[:, slot * 512:(slot + 1) * 512],
                                  psO2[:, slot * 512:(slot + 1) * 512])
                    return e.tensor_copy(fscr[:, 1:2], obuf[:, slot * 512 + 511:slot * 512 + 512])
                tOev[slot] = q2.on("vector", oev, after=[tMM2, tOst[slot]])

                def ost(e, sem, ht=ht, tc=tc, slot=slot):
                    e.dma_start(out=out_d[128 * ht:128 * (ht + 1), 512 * tc:512 * (tc + 1)],
                                in_=obuf[:, slot * 512:(slot + 1) * 512]).then_inc(sem, 16)
                tOst[slot] = q2.on("sync", ost, after=[tOev[slot]], ndma=1)

        q2.emit(block)
        q2.finish(block, [tOst[0], tOst[1]])
    return nc


def _pi_n():
    p = []
    for a in range(64):
        p.extend(range(256 * a, 256 * a + 256, 2))
        p.extend(range(256 * a + 1, 256 * a + 256, 2))
    return np.array(p, dtype=np.int64)


def _pi_h():
    p = []
    for a in range(8):
        p.extend(range(256 * a, 256 * a + 256, 2))
        p.extend(range(256 * a + 1, 256 * a + 256, 2))
    return np.array(p, dtype=np.int64)


def host_l0_ffn(inputs, x1b_parts):
    pn = _pi_n()
    ph = _pi_h()
    gp = np.asarray(inputs["L0_ffn_gate_W_packed"])
    up = np.asarray(inputs["L0_ffn_up_W_packed"])
    dp = np.asarray(inputs["L0_ffn_down_W_packed"])
    gs = np.asarray(inputs["L0_ffn_gate_scales"], np.float32)
    us = np.asarray(inputs["L0_ffn_up_scales"], np.float32)
    ds = np.asarray(inputs["L0_ffn_down_scales"], np.float32)
    bg = np.ascontiguousarray(gp.view(np.uint8).reshape(16384, 1024, 4)[:, :, 0][pn].T)
    bu = np.ascontiguousarray(up.view(np.uint8).reshape(16384, 1024, 4)[:, :, 0][pn].T)
    bd = np.ascontiguousarray(dp.view(np.uint8).reshape(2048, 8192, 4)[:, :, 0].T)
    gsT = np.ascontiguousarray(gs[pn].T)
    usT = np.ascontiguousarray(us[pn].T)
    dsT = np.ascontiguousarray(ds.T)
    x1b = np.concatenate([np.asarray(p) for p in x1b_parts], axis=0)
    xT = np.ascontiguousarray(x1b.T[ph])
    in_maps = []
    for c in range(NCORES):
        in_maps.append({
            "xT": xT,
            "tg": np.ascontiguousarray(bg[:, 2048 * c:2048 * (c + 1)]),
            "tu": np.ascontiguousarray(bu[:, 2048 * c:2048 * (c + 1)]),
            "td": np.ascontiguousarray(bd[1024 * c:1024 * (c + 1), :]),
            "gsT": np.ascontiguousarray(gsT[:, 2048 * c:2048 * (c + 1)]),
            "usT": np.ascontiguousarray(usT[:, 2048 * c:2048 * (c + 1)]),
            "dsT": np.ascontiguousarray(dsT[64 * c:64 * (c + 1), :]),
        })
    return in_maps



def build_reduce_ln(npart=8, emit_bf16=True, nresid=1):
    nc = bass.Bass("TRN2", target_bir_lowering=False, debug=False)
    p_d = [nc.dram_tensor(f"p{i}", [1024, H], BF16, kind="ExternalInput").ap() for i in range(npart)]
    x_d = nc.dram_tensor("x", [1024, H], F32, kind="ExternalInput").ap()
    x2_d = nc.dram_tensor("x2r", [1024, H], F32, kind="ExternalInput").ap() if nresid == 2 else None
    yf_d = nc.dram_tensor("yf", [1024, H], F32, kind="ExternalOutput").ap()
    yb_d = nc.dram_tensor("yb", [1024, H], BF16, kind="ExternalOutput").ap() if emit_bf16 else None

    with ExitStack() as es:
        sb = lambda n, sh, dt: es.enter_context(nc.sbuf_tensor(n, sh, dt))
        pt = sb("pt", [128, 2 * npart * H], BF16)
        xres = sb("xres", [128, 2 * H], F32)
        xres2 = sb("xres2", [128, 2 * H], F32) if nresid == 2 else None
        acc = sb("acc", [128, H], F32)
        xo = sb("xo", [128, H], F32)
        xc = sb("xc", [128, H], F32)
        scrap = sb("scrap", [128, H], F32)
        yf = sb("yf_s", [128, 2 * H], F32)
        yb = sb("yb_s", [128, 2 * H], BF16)
        cols = sb("cols", [128, 8], F32)
        fscr = sb("fscr", [128, 8], F32)
        block = es.enter_context(nc.Block())
        q = Seq(es, nc)

        tSum = [None] * 8
        tStore = [None] * 8
        for jt in range(8):
            slot = jt % 2

            def ld(e, sem, jt=jt, slot=slot):
                for i in range(npart):
                    e.dma_start(out=pt[:, (slot * npart + i) * H:(slot * npart + i + 1) * H],
                                in_=p_d[i][128 * jt:128 * (jt + 1), :]).then_inc(sem, 16)
                e.dma_start(out=xres[:, slot * H:(slot + 1) * H],
                            in_=x_d[128 * jt:128 * (jt + 1), :]).then_inc(sem, 16)
                if nresid == 2:
                    e.dma_start(out=xres2[:, slot * H:(slot + 1) * H],
                                in_=x2_d[128 * jt:128 * (jt + 1), :]).then_inc(sem, 16)
            tL = q.on("sync", ld, after=[tSum[jt - 2] if jt >= 2 else None], ndma=npart + nresid)

            def vsum(e, jt=jt, slot=slot):
                base = slot * npart * H
                e.tensor_tensor(acc[:], pt[:, base:base + H], pt[:, base + H:base + 2 * H], ALU.add)
                for i in range(2, npart):
                    e.tensor_tensor(acc[:], acc[:], pt[:, base + i * H:base + (i + 1) * H], ALU.add)
                if nresid == 2:
                    e.tensor_tensor(acc[:], acc[:], xres2[:, slot * H:(slot + 1) * H], ALU.add)
                e.scalar_tensor_tensor(xo[:], acc[:], 1.0, xres[:, slot * H:(slot + 1) * H],
                                       ALU.mult, ALU.add, accum_out=cols[:, 0:1])
                return e.tensor_copy(fscr[:, 1:2], cols[:, 0:1])
            tSum[jt] = q.on("vector", vsum, after=[tL])

            def apply(e, jt=jt, slot=slot):
                e.activation(yf[:, slot * H:(slot + 1) * H], xc[:], AF.Copy, scale=cols[:, 5:6])
                if emit_bf16:
                    e.copy(yb[:, slot * H:(slot + 1) * H], yf[:, slot * H:(slot + 1) * H])
                    e.copy(fscr[:, 0:1], yb[:, slot * H + H - 1:slot * H + H])
                return e.copy(fscr[:, 0:1], yf[:, slot * H + H - 1:slot * H + H])
            tAPP = ln_zigzag(q, tSum[jt], xo, xc, scrap, cols, fscr, 0, H, 1e-5, apply,
                             extra_after=[tStore[jt - 2] if jt >= 2 else None])

            def st(e, sem, jt=jt, slot=slot):
                e.dma_start(out=yf_d[128 * jt:128 * (jt + 1), :],
                            in_=yf[:, slot * H:(slot + 1) * H]).then_inc(sem, 16)
                if emit_bf16:
                    e.dma_start(out=yb_d[128 * jt:128 * (jt + 1), :],
                                in_=yb[:, slot * H:(slot + 1) * H]).then_inc(sem, 16)
            tStore[jt] = q.on("sync", st, after=[tAPP], ndma=2 if emit_bf16 else 1)

        q.emit(block)
        q.finish(block, [tStore[6], tStore[7]])
    return nc


def host_reduce_ln(partials, x_parts, token_major=False, x2_parts=None):
    in_maps = []
    for c in range(NCORES):
        m = {}
        for i, p in enumerate(partials):
            p = np.asarray(p)
            if token_major:
                m[f"p{i}"] = np.ascontiguousarray(p[1024 * c:1024 * (c + 1), :])
            else:
                m[f"p{i}"] = np.ascontiguousarray(p[:, 1024 * c:1024 * (c + 1)].T)
        m["x"] = np.ascontiguousarray(np.asarray(x_parts[c]))
        if x2_parts is not None:
            m["x2r"] = np.ascontiguousarray(np.asarray(x2_parts[c]))
        in_maps.append(m)
    return in_maps



RMS_EPS_L1 = float(np.finfo(np.float32).eps)


def build_l1_qkv():
    nc = bass.Bass("TRN2", target_bir_lowering=False, debug=False)
    xT_d = nc.dram_tensor("xT", [H, S], F32, kind="ExternalInput").ap()
    wq_d = nc.dram_tensor("wq", [H, 1024], F32, kind="ExternalInput").ap()
    wk_d = nc.dram_tensor("wk", [H, 256], F32, kind="ExternalInput").ap()
    wv_d = nc.dram_tensor("wv", [H, 256], F32, kind="ExternalInput").ap()
    qg_d = nc.dram_tensor("qg", [1, 8], F32, kind="ExternalInput").ap()
    cos_d = nc.dram_tensor("cosq", [128, 16 * 64], F32, kind="ExternalInput").ap()
    sin_d = nc.dram_tensor("sinq", [128, 16 * 64], F32, kind="ExternalInput").ap()
    id_d = nc.dram_tensor("ident", [128, 128], BF16, kind="ExternalInput").ap()
    qT_d = nc.dram_tensor("qT", [1024, S], BF16, kind="ExternalOutput").ap()
    kT_d = nc.dram_tensor("kT", [256, S], BF16, kind="ExternalOutput").ap()
    vt_d = nc.dram_tensor("vtok", [S, 256], BF16, kind="ExternalOutput").ap()

    with ExitStack() as es:
        sb = lambda n, sh, dt: es.enter_context(nc.sbuf_tensor(n, sh, dt))
        xT = sb("xT_s", [128, 16 * S], BF16)
        wb = sb("wb", [128, 16 * 1024], BF16)
        cosq = sb("cosq_s", [128, 16 * 64], F32)
        sinq = sb("sinq_s", [128, 16 * 64], F32)
        qgc = sb("qgc", [128, 8], F32)
        qn = sb("qn", [128, 1024], F32)
        qro = sb("qro", [128, 16 * 1024], BF16)
        kro = sb("kro", [128, 16 * 256], BF16)
        vo = sb("vo", [128, 2 * 256], BF16)
        orow = sb("orow", [128, S], BF16)
        sqc = sb("sqc", [128, 32], F32)
        scrap = sb("scrap", [128, 128], F32)
        fscr = sb("fscr", [128, 8], F32)
        idt = sb("idt_s", [128, 128], BF16)
        psQ = es.enter_context(nc.psum_tensor("psQ", [128, 1024], F32))
        psT = es.enter_context(nc.psum_tensor("psT", [128, 1024], BF16))
        block = es.enter_context(nc.Block())
        q = Seq(es, nc)

        def ld(e, sem):
            e.dma_start(out=cosq[:], in_=cos_d).then_inc(sem, 16)
            e.dma_start(out=sinq[:], in_=sin_d).then_inc(sem, 16)
            e.dma_start(out=idt[:], in_=id_d).then_inc(sem, 16)
            e.dma_start(out=qgc[:], in_=qg_d.broadcast_to([128, 8])).then_inc(sem, 16)
        tLD = q.on("sync", ld, ndma=4)

        def ldx(e, sem):
            for i in range(16):
                e.dma_start(out=xT[:, i * S:(i + 1) * S],
                            in_=xT_d[128 * i:128 * (i + 1), :]).then_inc(sem, 16)
        tX = q.on("gpsimd", ldx, ndma=16)

        def emit_proj_norm_rope(w_d, ncols, nheads, out_tiles, tWprev, tPrevUse, eps):
            def ldw(e, sem):
                for i in range(16):
                    e.dma_start(out=wb[:, i * ncols:(i + 1) * ncols],
                                in_=w_d[128 * i:128 * (i + 1), :]).then_inc(sem, 16)
            tW = q.on("gpsimd", ldw, after=[tPrevUse], ndma=16)

            toks = [None] * 16
            tMM = None
            tLastMM = None
            for j in range(16):
                def pmm(e, j=j, ncols=ncols):
                    nch = max(1, ncols // 512)
                    for n in range(nch):
                        w512 = min(512, ncols)
                        for i in range(16):
                            ins = e.matmul(psQ[:, n * 512:n * 512 + w512],
                                           xT[:, i * S + j * 128:i * S + j * 128 + 128],
                                           wb[:, i * ncols + n * w512:i * ncols + (n + 1) * w512],
                                           start=(i == 0), stop=(i == 15))
                    return ins
                tMM = q.on("tensor", pmm, after=[tX, tW, toks[j - 1] if j else None])
                tLastMM = tMM

                def ssq(e, j=j, nheads=nheads):
                    for lh in range(nheads):
                        ins = e.activation(scrap[:], psQ[:, lh * 128:(lh + 1) * 128], AF.Square,
                                           accum_out=sqc[:, lh:lh + 1])
                    return e.copy(fscr[:, 0:1], sqc[:, nheads - 1:nheads])
                t1 = q.on("scalar", ssq, after=[tMM])

                def vw(e, nheads=nheads, eps=eps):
                    e.tensor_scalar(sqc[:, 8:8 + nheads], sqc[:, 0:nheads], 1.0 / 128.0, eps,
                                    ALU.mult, ALU.add)
                    return e.tensor_copy(fscr[:, 1:2], sqc[:, 8 + nheads - 1:8 + nheads])
                t2 = q.on("vector", vw, after=[t1])

                def s3(e, nheads=nheads):
                    e.activation(sqc[:, 16:16 + nheads], sqc[:, 8:8 + nheads], AF.Sqrt)
                    return e.copy(fscr[:, 0:1], sqc[:, 16 + nheads - 1:16 + nheads])
                t3 = q.on("scalar", s3, after=[t2])

                def v4(e, nheads=nheads):
                    e.reciprocal(sqc[:, 24:24 + nheads], sqc[:, 16:16 + nheads])
                    return e.tensor_copy(fscr[:, 1:2], sqc[:, 24 + nheads - 1:24 + nheads])
                t4 = q.on("vector", v4, after=[t3])

                def s5(e, nheads=nheads):
                    for lh in range(nheads):
                        ins = e.activation(qn[:, lh * 128:(lh + 1) * 128],
                                           psQ[:, lh * 128:(lh + 1) * 128], AF.Copy,
                                           scale=sqc[:, 24 + lh:25 + lh])
                    return e.copy(fscr[:, 0:1], qn[:, nheads * 128 - 1:nheads * 128])
                t5 = q.on("scalar", s5, after=[t4])

                def vrope(e, j=j, nheads=nheads, ncols=ncols):
                    c = cosq[:, j * 64:(j + 1) * 64]
                    s = sinq[:, j * 64:(j + 1) * 64]
                    for lh in range(nheads):
                        a = qn[:, lh * 128:lh * 128 + 64]
                        b2 = qn[:, lh * 128 + 64:lh * 128 + 128]
                        e.tensor_tensor(scrap[:, 0:64], a, c, ALU.mult)
                        e.tensor_tensor(scrap[:, 64:128], b2, s, ALU.mult)
                        e.tensor_tensor(out_tiles[:, j * ncols + lh * 128:j * ncols + lh * 128 + 64],
                                        scrap[:, 0:64], scrap[:, 64:128], ALU.add)
                        e.tensor_tensor(scrap[:, 0:64], b2, c, ALU.mult)
                        e.tensor_tensor(scrap[:, 64:128], a, s, ALU.mult)
                        e.tensor_tensor(out_tiles[:, j * ncols + lh * 128 + 64:j * ncols + (lh + 1) * 128],
                                        scrap[:, 0:64], scrap[:, 64:128], ALU.subtract)
                    return e.tensor_copy(fscr[:, 1:2],
                                         out_tiles[:, (j + 1) * ncols - 1:(j + 1) * ncols])
                toks[j] = q.on("vector", vrope, after=[t5, tLD])
            return tLastMM, toks

        tQuse, tQtok = emit_proj_norm_rope(wq_d, 1024, 8, qro, None, None, RMS_EPS_L1)
        tKuse, tKtok = emit_proj_norm_rope(wk_d, 256, 2, kro, None, tQuse, RMS_EPS_L1)

        tTr = None
        tQst = None
        for a in range(8):
            for half in range(2):
                def trq(e, a=a, half=half):
                    for jj in range(8):
                        j = 8 * half + jj
                        ins = e.transpose(psT[:, jj * 128:(jj + 1) * 128],
                                          qro[:, j * 1024 + a * 128:j * 1024 + (a + 1) * 128], idt[:])
                    return ins
                tt = q.on("tensor", trq, after=[tQtok[15], tTr])

                def evq(e, a=a, half=half):
                    for jj in range(8):
                        ins = e.copy(orow[:, half * 1024 + jj * 128:half * 1024 + (jj + 1) * 128],
                                     psT[:, jj * 128:(jj + 1) * 128])
                    return e.copy(fscr[:, 2:3], orow[:, half * 1024 + 1023:half * 1024 + 1024])
                tTr = q.on("scalar", evq, after=[tt, tQst if half == 0 else None])

            def sgq(e, a=a):
                e.tensor_scalar(orow[:], orow[:], qgc[:, a:a + 1], None, ALU.mult)
                return e.tensor_copy(fscr[:, 3:4], orow[:, S - 1:S])
            tG = q.on("vector", sgq, after=[tTr])

            def stq(e, sem, a=a):
                e.dma_start(out=qT_d[a * 128:(a + 1) * 128, :], in_=orow[:]).then_inc(sem, 16)
            tQst = q.on("sync", stq, after=[tG], ndma=1)

        for a in range(2):
            for half in range(2):
                def trk2(e, a=a, half=half):
                    for jj in range(8):
                        j = 8 * half + jj
                        ins = e.transpose(psT[:, jj * 128:(jj + 1) * 128],
                                          kro[:, j * 256 + a * 128:j * 256 + (a + 1) * 128], idt[:])
                    return ins
                tt = q.on("tensor", trk2, after=[tKtok[15], tTr])

                def evk2(e, a=a, half=half):
                    for jj in range(8):
                        ins = e.copy(orow[:, half * 1024 + jj * 128:half * 1024 + (jj + 1) * 128],
                                     psT[:, jj * 128:(jj + 1) * 128])
                    return e.copy(fscr[:, 2:3], orow[:, half * 1024 + 1023:half * 1024 + 1024])
                tTr = q.on("scalar", evk2, after=[tt, tQst if half == 0 else None])

            def stk(e, sem, a=a):
                e.dma_start(out=kT_d[a * 128:(a + 1) * 128, :], in_=orow[:]).then_inc(sem, 16)
            tQst = q.on("sync", stk, after=[tTr], ndma=1)

        def ldwv(e, sem):
            for i in range(16):
                e.dma_start(out=wb[:, i * 256:(i + 1) * 256],
                            in_=wv_d[128 * i:128 * (i + 1), :]).then_inc(sem, 16)
        tWv = q.on("gpsimd", ldwv, after=[tKuse], ndma=16)

        tVev = None
        tVst = [None, None]
        for j in range(16):
            def vmm(e, j=j):
                for i in range(16):
                    ins = e.matmul(psQ[:, 0:256],
                                   xT[:, i * S + j * 128:i * S + j * 128 + 128],
                                   wb[:, i * 256:(i + 1) * 256],
                                   start=(i == 0), stop=(i == 15))
                return ins
            tV = q.on("tensor", vmm, after=[tWv, tVev])

            def vev(e, j=j):
                e.tensor_copy(vo[:, (j % 2) * 256:(j % 2 + 1) * 256], psQ[:, 0:256])
                return e.tensor_copy(fscr[:, 4:5], vo[:, (j % 2) * 256 + 255:(j % 2) * 256 + 256])
            tVev = q.on("vector", vev, after=[tV, tVst[j % 2]])

            def stv(e, sem, j=j):
                e.dma_start(out=vt_d[128 * j:128 * (j + 1), :],
                            in_=vo[:, (j % 2) * 256:(j % 2 + 1) * 256]).then_inc(sem, 16)
            tVst[j % 2] = q.on("sync", stv, after=[tVev], ndma=1)

        q.emit(block)
        q.finish(block, [tQst, tVst[0], tVst[1]])
    return nc


def host_l1_qkv(inputs, x2b_parts):
    x2b = np.concatenate([np.asarray(p) for p in x2b_parts], 0).reshape(B, S, H)
    wqT = np.ascontiguousarray(np.asarray(inputs["L1_attn_Wq"], np.float32).T)
    wkT = np.ascontiguousarray(np.asarray(inputs["L1_attn_Wk"], np.float32).T)
    wvT = np.ascontiguousarray(np.asarray(inputs["L1_attn_Wv"], np.float32).T)
    qg = np.asarray(inputs["L1_attn_qg"], np.float32).reshape(16)
    cosq, sinq = _l1_costab()
    cost = _tiles128(cosq)
    sint = _tiles128(sinq)
    in_maps = []
    for c in range(NCORES):
        b, kp = c // 2, c % 2
        in_maps.append({
            "xT": np.ascontiguousarray(x2b[b].astype(np.float32).T),
            "wq": np.ascontiguousarray(wqT[:, 1024 * kp:1024 * (kp + 1)]),
            "wk": np.ascontiguousarray(wkT[:, 256 * kp:256 * (kp + 1)]),
            "wv": np.ascontiguousarray(wvT[:, 256 * kp:256 * (kp + 1)]),
            "qg": np.ascontiguousarray(qg[8 * kp:8 * (kp + 1)].reshape(1, 8)),
            "cosq": cost, "sinq": sint,
            "ident": IDENT,
        })
    return in_maps



def build_causal_attn(kvc, kvdiv):
    nc = bass.Bass("TRN2", target_bir_lowering=False, debug=False)
    qT_d = nc.dram_tensor("qT", [128, 8 * S], BF16, kind="ExternalInput").ap()
    kT_d = nc.dram_tensor("kT", [128, (kvc // 128) * S], BF16, kind="ExternalInput").ap()
    vt_d = nc.dram_tensor("vtok", [128, 16 * kvc], BF16, kind="ExternalInput").ap()
    mk_d = nc.dram_tensor("mask", [128, 128], F32, kind="ExternalInput").ap()
    id_d = nc.dram_tensor("ident", [128, 128], BF16, kind="ExternalInput").ap()
    y_d = nc.dram_tensor("ytok", [S, 1024], BF16, kind="ExternalOutput").ap()
    SCALE = float(1.0 / np.sqrt(128.0))

    with ExitStack() as es:
        sb = lambda n, sh, dt: es.enter_context(nc.sbuf_tensor(n, sh, dt))
        qT = sb("qT_s", [128, 8 * S], BF16)
        kT = sb("kT_s", [128, (kvc // 128) * S], BF16)
        vtok = sb("vt_s", [128, 16 * kvc], BF16)
        comb = sb("comb", [128, 16 * 1024], BF16)
        qs = sb("qs", [128, S], BF16)
        SS = sb("SS", [128, S], BF16)
        P = sb("P", [128, S], BF16)
        PT = sb("PT", [128, 16 * 128], BF16)
        maskt = sb("maskt", [128, 128], F32)
        cols = sb("cols", [128, 8], F32)
        fscr = sb("fscr", [128, 8], F32)
        idt = sb("idt_s", [128, 128], BF16)
        psS = es.enter_context(nc.psum_tensor("psS", [128, 2048], F32))
        psT = es.enter_context(nc.psum_tensor("psT", [128, 1024], BF16))
        psA = es.enter_context(nc.psum_tensor("psA", [128, 512], F32))
        block = es.enter_context(nc.Block())
        q = Seq(es, nc)

        def ld(e, sem):
            e.dma_start(out=qT[:], in_=qT_d).then_inc(sem, 16)
            e.dma_start(out=kT[:], in_=kT_d).then_inc(sem, 16)
            e.dma_start(out=vtok[:], in_=vt_d).then_inc(sem, 16)
            e.dma_start(out=idt[:], in_=id_d).then_inc(sem, 16)
            e.dma_start(out=maskt[:], in_=mk_d).then_inc(sem, 16)
        tLD = q.on("sync", ld, ndma=5)

        tC = None
        tPTev = None
        tExPrev = None
        tWSPrev = None
        for lh in range(8):
            hOFF = lh * S
            kOFF = (lh // kvdiv) * S if kvc != 1024 else lh * S
            voff = (lh // kvdiv) * 128 if kvc != 1024 else lh * 128

            def qsv(e, hOFF=hOFF):
                e.tensor_scalar_mul(qs[:], qT[:, hOFF:hOFF + S], SCALE)
                return e.tensor_copy(fscr[:, 2:3], qs[:, S - 1:S])
            tQS = q.on("vector", qsv, after=[tLD, tC])

            for jq in range(16):
                kmax = (jq + 1) * 128

                def smm(e, jq=jq, kOFF=kOFF):
                    kx = (jq + 1) * 128
                    n = 0
                    while n * 512 < kx:
                        w = min(512, kx - n * 512)
                        ins = e.matmul(psS[:, n * 512:n * 512 + w],
                                       qs[:, jq * 128:(jq + 1) * 128],
                                       kT[:, kOFF + n * 512:kOFF + n * 512 + w],
                                       start=True, stop=True)
                        n += 1
                    return ins
                tS = q.on("tensor", smm, after=[tQS, tExPrev])

                def vss(e, jq=jq, kmax=kmax):
                    if jq:
                        e.tensor_copy(SS[:, 0:jq * 128], psS[:, 0:jq * 128])
                    e.scalar_tensor_tensor(SS[:, jq * 128:kmax], psS[:, jq * 128:kmax],
                                           1.0, maskt[:], ALU.mult, ALU.add)
                    e.reduce_max(cols[:, 0:1], SS[:, 0:kmax], axis=AX, negate=True)
                    return e.tensor_copy(cols[:, 1:2], cols[:, 0:1])
                tV1 = q.on("vector", vss, after=[tS])

                def sexp(e, kmax=kmax):
                    e.activation(P[:, 0:kmax], SS[:, 0:kmax], AF.Exp, bias=cols[:, 0:1],
                                 scale=1.0, accum_out=cols[:, 3:4])
                    e.copy(fscr[:, 6:7], P[:, kmax - 1:kmax])
                    return e.copy(fscr[:, 6:7], cols[:, 3:4])
                tEx = q.on("scalar", sexp, after=[tV1])
                tExPrev = tEx

                def vrec(e):
                    e.reciprocal(cols[:, 4:5], cols[:, 3:4])
                    return e.tensor_copy(cols[:, 5:6], cols[:, 4:5])
                tR = q.on("vector", vrec, after=[tEx])

                def swc(e):
                    e.copy(cols[:, 6:7], cols[:, 4:5])
                    return e.copy(fscr[:, 7:8], cols[:, 6:7])
                tWS = q.on("scalar", swc, after=[tR, tWSPrev, tC])
                tWSPrev = tWS

                nblk = jq + 1
                tPh = None
                for b0 in range(0, nblk, 8):
                    bn = min(8, nblk - b0)

                    def trp(e, b0=b0, bn=bn):
                        for i in range(bn):
                            ins = e.transpose(psT[:, i * 128:(i + 1) * 128],
                                              P[:, (b0 + i) * 128:(b0 + i + 1) * 128], idt[:])
                        return ins
                    tt = q.on("tensor", trp, after=[tEx, tPTev])

                    def evp(e, b0=b0, bn=bn):
                        for i in range(bn):
                            ins = e.copy(PT[:, (b0 + i) * 128:(b0 + i + 1) * 128],
                                         psT[:, i * 128:(i + 1) * 128])
                        return e.copy(fscr[:, 4:5], PT[:, (b0 + bn) * 128 - 1:(b0 + bn) * 128])
                    tPTev = q.on("scalar", evp, after=[tt])
                    tPh = tPTev

                def amm(e, jq=jq, voff=voff, nblk=nblk):
                    for j in range(nblk):
                        ins = e.matmul(psA[:, 0:128],
                                       PT[:, j * 128:(j + 1) * 128],
                                       vtok[:, j * kvc + voff:j * kvc + voff + 128],
                                       start=(j == 0), stop=(j == nblk - 1))
                    return ins
                tAm = q.on("tensor", amm, after=[tPh, tC])

                def combv(e, lh=lh, jq=jq):
                    e.tensor_scalar(comb[:, jq * 1024 + lh * 128:jq * 1024 + (lh + 1) * 128],
                                    psA[:, 0:128], cols[:, 6:7], None, ALU.mult)
                    return e.tensor_copy(fscr[:, 2:3],
                                         comb[:, jq * 1024 + (lh + 1) * 128 - 1:jq * 1024 + (lh + 1) * 128])
                tC = q.on("vector", combv, after=[tAm, tWS])

        def stc(e, sem):
            for j in range(16):
                e.dma_start(out=y_d[j * 128:(j + 1) * 128, :],
                            in_=comb[:, j * 1024:(j + 1) * 1024]).then_inc(sem, 16)
        tSt = q.on("sync", stc, after=[tC], ndma=16)

        q.emit(block)
        q.finish(block, [tSt])
    return nc


CAUSAL_MASK = np.triu(np.full((128, 128), -30000.0, dtype=np.float32), 1)


def host_causal_attn(qkv_results):
    in_maps = []
    for c in range(NCORES):
        r = qkv_results[c]
        in_maps.append({
            "qT": _tiles128(np.asarray(r["qT"])),
            "kT": _tiles128(np.asarray(r["kT"])),
            "vtok": _tiles128(np.asarray(r["vtok"])),
            "mask": CAUSAL_MASK,
            "ident": IDENT,
        })
    return in_maps



def build_oproj_block(apply_ln):
    nc = bass.Bass("TRN2", target_bir_lowering=False, debug=False)
    cbT_d = nc.dram_tensor("cbT", [128, 16 * 1024], BF16, kind="ExternalInput").ap()
    woT_d = nc.dram_tensor("woT", [H, H], F32, kind="ExternalInput").ap()
    x_d = nc.dram_tensor("x", [1024, H], F32, kind="ExternalInput").ap()
    lnw_d = nc.dram_tensor("lnw", [1, H], F32, kind="ExternalInput").ap()
    yf_d = nc.dram_tensor("yf", [1024, H], F32, kind="ExternalOutput").ap()
    yb_d = nc.dram_tensor("yb", [1024, H], BF16, kind="ExternalOutput").ap()
    rn_d = nc.dram_tensor("rn", [1024, H], BF16, kind="ExternalOutput").ap()

    with ExitStack() as es:
        sb = lambda n, sh, dt: es.enter_context(nc.sbuf_tensor(n, sh, dt))
        cbT = sb("cbT_s", [128, 16 * 1024], BF16)
        woT = sb("woT_s", [128, 16 * H], BF16)
        lnw = sb("lnw_s", [128, H], F32)
        xres = sb("xres", [128, 2 * H], F32)
        xo = sb("xo", [128, H], F32)
        xc = sb("xc", [128, H], F32)
        scrap = sb("scrap", [128, H], F32)
        yf = sb("yf_s", [128, 2 * H], F32)
        yb = sb("yb_s", [128, 2 * H], BF16)
        rn = sb("rn_s", [128, 2 * H], BF16)
        cols = sb("cols", [128, 16], F32)
        fscr = sb("fscr", [128, 8], F32)
        psO = es.enter_context(nc.psum_tensor("psO", [128, 2048], F32))
        block = es.enter_context(nc.Block())
        q = Seq(es, nc)

        def ld(e, sem):
            e.dma_start(out=cbT[:], in_=cbT_d).then_inc(sem, 16)
            e.dma_start(out=lnw[:], in_=lnw_d.broadcast_to([128, H])).then_inc(sem, 16)
        tLD = q.on("sync", ld, ndma=2)

        def ldw(e, sem):
            for i in range(16):
                e.dma_start(out=woT[:, i * H:(i + 1) * H],
                            in_=woT_d[128 * i:128 * (i + 1), :]).then_inc(sem, 16)
        tW = q.on("gpsimd", ldw, ndma=16)

        tXO = [None] * 8
        tStore = [None] * 8
        for jt in range(8):
            slot = jt % 2

            def ldx(e, sem, jt=jt, slot=slot):
                e.dma_start(out=xres[:, slot * H:(slot + 1) * H],
                            in_=x_d[128 * jt:128 * (jt + 1), :]).then_inc(sem, 16)
            tX = q.on("sync", ldx, after=[tXO[jt - 2] if jt >= 2 else None], ndma=1)

            def mm(e, jt=jt):
                for n in range(4):
                    for i in range(16):
                        ins = e.matmul(psO[:, n * 512:(n + 1) * 512],
                                       cbT[:, i * 1024 + jt * 128:i * 1024 + (jt + 1) * 128],
                                       woT[:, i * H + n * 512:i * H + (n + 1) * 512],
                                       start=(i == 0), stop=(i == 15))
                return ins
            tM = q.on("tensor", mm, after=[tLD, tW, tXO[jt - 1] if jt >= 1 else None])

            def v_xo(e, slot=slot):
                e.scalar_tensor_tensor(xo[:], psO[:], 1.0, xres[:, slot * H:(slot + 1) * H],
                                       ALU.mult, ALU.add, accum_out=cols[:, 0:1])
                return e.tensor_copy(fscr[:, 1:2], cols[:, 0:1])
            tXO[jt] = q.on("vector", v_xo, after=[tM, tX])

            if apply_ln:
                def apply(e, slot=slot):
                    e.activation(yf[:, slot * H:(slot + 1) * H], xc[:], AF.Copy,
                                 scale=cols[:, 5:6])
                    e.copy(yb[:, slot * H:(slot + 1) * H], yf[:, slot * H:(slot + 1) * H])
                    e.activation(scrap[:], yf[:, slot * H:(slot + 1) * H], AF.Square,
                                 accum_out=cols[:, 8:9])
                    return e.copy(fscr[:, 0:1], cols[:, 8:9])
                tA1 = ln_zigzag(q, tXO[jt], xo, xc, scrap, cols, fscr, 0, H, 1e-5, apply,
                                extra_after=[tStore[jt - 2] if jt >= 2 else None])
            else:
                def apply(e, slot=slot):
                    e.copy(yf[:, slot * H:(slot + 1) * H], xo[:])
                    e.copy(yb[:, slot * H:(slot + 1) * H], xo[:])
                    e.activation(scrap[:], xo[:], AF.Square, accum_out=cols[:, 8:9])
                    return e.copy(fscr[:, 0:1], cols[:, 8:9])
                tA1 = q.on("scalar", apply, after=[tXO[jt],
                                                  tStore[jt - 2] if jt >= 2 else None])

            def v_rw(e):
                e.tensor_scalar(cols[:, 9:10], cols[:, 8:9], 1.0 / H, 1e-5, ALU.mult, ALU.add)
                return e.tensor_copy(fscr[:, 1:2], cols[:, 9:10])
            t6 = q.on("vector", v_rw, after=[tA1])

            def s_rs(e):
                e.activation(cols[:, 10:11], cols[:, 9:10], AF.Sqrt)
                return e.copy(fscr[:, 0:1], cols[:, 10:11])
            t7 = q.on("scalar", s_rs, after=[t6])

            def v_rr(e):
                e.reciprocal(cols[:, 11:12], cols[:, 10:11])
                return e.tensor_copy(fscr[:, 1:2], cols[:, 11:12])
            t8 = q.on("vector", v_rr, after=[t7])

            def s_rn(e, slot=slot):
                e.activation(scrap[:], yf[:, slot * H:(slot + 1) * H], AF.Copy,
                             scale=cols[:, 11:12])
                return e.copy(fscr[:, 0:1], scrap[:, H - 1:H])
            t9 = q.on("scalar", s_rn, after=[t8])

            def v_rn(e, slot=slot):
                e.tensor_tensor(rn[:, slot * H:(slot + 1) * H], scrap[:], lnw[:], ALU.mult)
                return e.tensor_copy(fscr[:, 1:2], rn[:, slot * H + H - 1:slot * H + H])
            t10 = q.on("vector", v_rn, after=[t9, tLD])

            def st(e, sem, jt=jt, slot=slot):
                e.dma_start(out=yf_d[128 * jt:128 * (jt + 1), :],
                            in_=yf[:, slot * H:(slot + 1) * H]).then_inc(sem, 16)
                e.dma_start(out=yb_d[128 * jt:128 * (jt + 1), :],
                            in_=yb[:, slot * H:(slot + 1) * H]).then_inc(sem, 16)
                e.dma_start(out=rn_d[128 * jt:128 * (jt + 1), :],
                            in_=rn[:, slot * H:(slot + 1) * H]).then_inc(sem, 16)
            tStore[jt] = q.on("sync", st, after=[t10], ndma=3)

        q.emit(block)
        q.finish(block, [tStore[6], tStore[7]])
    return nc


def host_oproj_block(attn_tok_results, colmaps, woT, x_parts, lnw):
    y = np.zeros((B, S, H), dtype=BF)
    for c in range(NCORES):
        b, p = c // 2, c % 2
        y[b][:, colmaps[p]] = np.asarray(attn_tok_results[c])
    in_maps = []
    for c in range(NCORES):
        b, half = c // 2, c % 2
        sl = slice(half * 1024, (half + 1) * 1024)
        in_maps.append({
            "cbT": _tiles128(np.ascontiguousarray(y[b][sl].T)),
            "woT": woT,
            "x": np.ascontiguousarray(np.asarray(x_parts[c])),
            "lnw": np.ascontiguousarray(np.asarray(lnw, np.float32).reshape(1, H)),
        })
    return in_maps



def build_mla_proj():
    nc = bass.Bass("TRN2", target_bir_lowering=False, debug=False)
    xT_d = nc.dram_tensor("xT", [H, S], BF16, kind="ExternalInput").ap()
    kvd_d = nc.dram_tensor("kvd", [H, 512], F32, kind="ExternalInput").ap()
    qd_d = nc.dram_tensor("qd", [H, 512], F32, kind="ExternalInput").ap()
    ku_d = nc.dram_tensor("ku", [512, 512], F32, kind="ExternalInput").ap()
    qu_d = nc.dram_tensor("qu", [512, 512], F32, kind="ExternalInput").ap()
    vu_d = nc.dram_tensor("vu", [512, 1024], F32, kind="ExternalInput").ap()
    rk_d = nc.dram_tensor("rk", [H, 512], F32, kind="ExternalInput").ap()
    rq_d = nc.dram_tensor("rq", [512, 512], F32, kind="ExternalInput").ap()
    cos_d = nc.dram_tensor("cosm", [128, 16 * 32], F32, kind="ExternalInput").ap()
    sin_d = nc.dram_tensor("sinm", [128, 16 * 32], F32, kind="ExternalInput").ap()
    id_d = nc.dram_tensor("ident", [128, 128], BF16, kind="ExternalInput").ap()
    qT_d = nc.dram_tensor("qT", [1024, S], BF16, kind="ExternalOutput").ap()
    kT_d = nc.dram_tensor("kT", [1024, S], BF16, kind="ExternalOutput").ap()
    vt_d = nc.dram_tensor("vtok", [S, 1024], BF16, kind="ExternalOutput").ap()

    with ExitStack() as es:
        sb = lambda n, sh, dt: es.enter_context(nc.sbuf_tensor(n, sh, dt))
        xcol = sb("xcol", [128, 16 * 512], BF16)
        kvdb = sb("kvdb", [128, 16 * 512], BF16)
        qdb = sb("qdb", [128, 16 * 512], BF16)
        kvdT = sb("kvdT", [128, 4 * S], BF16)
        qdT = sb("qdT", [128, 4 * S], BF16)
        kub = sb("kub", [128, 4 * 512], BF16)
        qub = sb("qub", [128, 4 * 512], BF16)
        vub = sb("vub", [128, 4 * 1024], BF16)
        rkb = sb("rkb", [128, 16 * 512], BF16)
        rqb = sb("rqb", [128, 4 * 512], BF16)
        ktot = sb("ktot", [128, 16 * 1024], BF16)
        qtot = sb("qtot", [128, 16 * 1024], BF16)
        vtot = sb("vtot", [128, 2 * 1024], BF16)
        cosm = sb("cosm_s", [128, 16 * 32], F32)
        sinm = sb("sinm_s", [128, 16 * 32], F32)
        rf = sb("rf", [128, 512], F32)
        rs = sb("rs", [128, 64], F32)
        orow = sb("orow", [128, S], BF16)
        fscr = sb("fscr", [128, 8], F32)
        idt = sb("idt_s", [128, 128], BF16)
        psA = es.enter_context(nc.psum_tensor("psA", [128, 512], F32))
        psB = es.enter_context(nc.psum_tensor("psB", [128, 512], F32))
        psV = es.enter_context(nc.psum_tensor("psV", [128, 1024], F32))
        psT = es.enter_context(nc.psum_tensor("psT", [128, 1024], BF16))
        block = es.enter_context(nc.Block())
        q = Seq(es, nc)

        def ld(e, sem):
            e.dma_start(out=cosm[:], in_=cos_d).then_inc(sem, 16)
            e.dma_start(out=sinm[:], in_=sin_d).then_inc(sem, 16)
            e.dma_start(out=idt[:], in_=id_d).then_inc(sem, 16)
        tLD = q.on("sync", ld, ndma=3)

        def ldw(e, sem):
            for i in range(16):
                e.dma_start(out=kvdb[:, i * 512:(i + 1) * 512],
                            in_=kvd_d[128 * i:128 * (i + 1), :]).then_inc(sem, 16)
            for i in range(16):
                e.dma_start(out=qdb[:, i * 512:(i + 1) * 512],
                            in_=qd_d[128 * i:128 * (i + 1), :]).then_inc(sem, 16)
            for i in range(16):
                e.dma_start(out=rkb[:, i * 512:(i + 1) * 512],
                            in_=rk_d[128 * i:128 * (i + 1), :]).then_inc(sem, 16)
            for lt in range(4):
                e.dma_start(out=kub[:, lt * 512:(lt + 1) * 512],
                            in_=ku_d[128 * lt:128 * (lt + 1), :]).then_inc(sem, 16)
                e.dma_start(out=qub[:, lt * 512:(lt + 1) * 512],
                            in_=qu_d[128 * lt:128 * (lt + 1), :]).then_inc(sem, 16)
                e.dma_start(out=vub[:, lt * 1024:(lt + 1) * 1024],
                            in_=vu_d[128 * lt:128 * (lt + 1), :]).then_inc(sem, 16)
                e.dma_start(out=rqb[:, lt * 512:(lt + 1) * 512],
                            in_=rq_d[128 * lt:128 * (lt + 1), :]).then_inc(sem, 16)
        tW = q.on("gpsimd", ldw, ndma=64)

        tXc = None
        tEvPrev = None
        tRopeK = None
        tKrLast = None
        for sc in range(4):
            def ldxc(e, sem, sc=sc):
                for i in range(16):
                    e.dma_start(out=xcol[:, i * 512:(i + 1) * 512],
                                in_=xT_d[128 * i:128 * (i + 1), 512 * sc:512 * (sc + 1)]).then_inc(sem, 16)
            tXc = q.on("sync", ldxc, after=[tEvPrev, tKrLast], ndma=16)

            for which, (wsrc, dstT) in enumerate([(kvdb, kvdT), (qdb, qdT)]):
                for lt in range(4):
                    def dmm(e, sc=sc, lt=lt, wsrc=wsrc):
                        for i in range(16):
                            ins = e.matmul(psA[:, 0:512],
                                           wsrc[:, i * 512 + lt * 128:i * 512 + (lt + 1) * 128],
                                           xcol[:, i * 512:(i + 1) * 512],
                                           start=(i == 0), stop=(i == 15))
                        return ins
                    tM = q.on("tensor", dmm, after=[tXc, tW, tEvPrev])

                    def ev(e, sc=sc, lt=lt, dstT=dstT):
                        e.tensor_copy(dstT[:, lt * S + sc * 512:lt * S + (sc + 1) * 512], psA[:, 0:512])
                        return e.tensor_copy(fscr[:, 0:1],
                                             dstT[:, lt * S + (sc + 1) * 512 - 1:lt * S + (sc + 1) * 512])
                    tEvPrev = q.on("vector", ev, after=[tM])

            for jj in range(4):
                j = sc * 4 + jj

                def krmm(e, jj=jj):
                    for i in range(16):
                        ins = e.matmul(psB[:, 0:512],
                                       xcol[:, i * 512 + jj * 128:i * 512 + (jj + 1) * 128],
                                       rkb[:, i * 512:(i + 1) * 512],
                                       start=(i == 0), stop=(i == 15))
                    return ins
                tKr = q.on("tensor", krmm, after=[tXc, tW, tRopeK])
                tKrLast = tKr

                def evkr(e):
                    e.tensor_copy(rf[:], psB[:, 0:512])
                    return e.tensor_copy(fscr[:, 1:2], rf[:, 511:512])
                tEvK = q.on("vector", evkr, after=[tKr])

                def ropek(e, j=j):
                    c = cosm[:, j * 32:(j + 1) * 32]
                    s = sinm[:, j * 32:(j + 1) * 32]
                    for n in range(8):
                        x1 = rf[:, n * 64:n * 64 + 32]
                        x2 = rf[:, n * 64 + 32:n * 64 + 64]
                        e.tensor_tensor(rs[:, 0:32], x1, c, ALU.mult)
                        e.tensor_tensor(rs[:, 32:64], x2, s, ALU.mult)
                        e.tensor_tensor(ktot[:, j * 1024 + n * 128 + 64:j * 1024 + n * 128 + 96],
                                        rs[:, 0:32], rs[:, 32:64], ALU.subtract)
                        e.tensor_tensor(rs[:, 0:32], x2, c, ALU.mult)
                        e.tensor_tensor(rs[:, 32:64], x1, s, ALU.mult)
                        e.tensor_tensor(ktot[:, j * 1024 + n * 128 + 96:j * 1024 + (n + 1) * 128],
                                        rs[:, 0:32], rs[:, 32:64], ALU.add)
                    return e.tensor_copy(fscr[:, 1:2], ktot[:, (j + 1) * 1024 - 1:(j + 1) * 1024])
                tRopeK = q.on("vector", ropek, after=[tEvK, tLD])

        tVst = [None, None]
        tPrevA = tRopeK
        tPrevB = tRopeK
        tPrevV = None
        for j in range(16):
            def kpmm(e, j=j):
                for lt in range(4):
                    ins = e.matmul(psA[:, 0:512],
                                   kvdT[:, lt * S + j * 128:lt * S + (j + 1) * 128],
                                   kub[:, lt * 512:(lt + 1) * 512],
                                   start=(lt == 0), stop=(lt == 3))
                return ins
            tKp = q.on("tensor", kpmm, after=[tEvPrev, tPrevA])

            def evkp(e, j=j):
                for n in range(8):
                    e.tensor_copy(ktot[:, j * 1024 + n * 128:j * 1024 + n * 128 + 64],
                                  psA[:, n * 64:(n + 1) * 64])
                return e.tensor_copy(fscr[:, 2:3], ktot[:, j * 1024 + 7 * 128 + 63:j * 1024 + 7 * 128 + 64])
            tPrevA = q.on("vector", evkp, after=[tKp])

            def qpmm(e, j=j):
                for lt in range(4):
                    ins = e.matmul(psB[:, 0:512],
                                   qdT[:, lt * S + j * 128:lt * S + (j + 1) * 128],
                                   qub[:, lt * 512:(lt + 1) * 512],
                                   start=(lt == 0), stop=(lt == 3))
                return ins
            tQp = q.on("tensor", qpmm, after=[tPrevB])

            def evqp(e, j=j):
                for n in range(8):
                    e.tensor_copy(qtot[:, j * 1024 + n * 128:j * 1024 + n * 128 + 64],
                                  psB[:, n * 64:(n + 1) * 64])
                return e.tensor_copy(fscr[:, 3:4], qtot[:, j * 1024 + 7 * 128 + 63:j * 1024 + 7 * 128 + 64])
            tPrevB = q.on("vector", evqp, after=[tQp])

            def qrmm(e, j=j):
                for lt in range(4):
                    ins = e.matmul(psB[:, 0:512],
                                   qdT[:, lt * S + j * 128:lt * S + (j + 1) * 128],
                                   rqb[:, lt * 512:(lt + 1) * 512],
                                   start=(lt == 0), stop=(lt == 3))
                return ins
            tQr = q.on("tensor", qrmm, after=[tPrevB])

            def evqr(e):
                e.tensor_copy(rf[:], psB[:, 0:512])
                return e.tensor_copy(fscr[:, 1:2], rf[:, 511:512])
            tEvQr = q.on("vector", evqr, after=[tQr])
            tPrevB = tEvQr

            def ropeq(e, j=j):
                c = cosm[:, j * 32:(j + 1) * 32]
                s = sinm[:, j * 32:(j + 1) * 32]
                for n in range(8):
                    x1 = rf[:, n * 64:n * 64 + 32]
                    x2 = rf[:, n * 64 + 32:n * 64 + 64]
                    e.tensor_tensor(rs[:, 0:32], x1, c, ALU.mult)
                    e.tensor_tensor(rs[:, 32:64], x2, s, ALU.mult)
                    e.tensor_tensor(qtot[:, j * 1024 + n * 128 + 64:j * 1024 + n * 128 + 96],
                                    rs[:, 0:32], rs[:, 32:64], ALU.subtract)
                    e.tensor_tensor(rs[:, 0:32], x2, c, ALU.mult)
                    e.tensor_tensor(rs[:, 32:64], x1, s, ALU.mult)
                    e.tensor_tensor(qtot[:, j * 1024 + n * 128 + 96:j * 1024 + (n + 1) * 128],
                                    rs[:, 0:32], rs[:, 32:64], ALU.add)
                return e.tensor_copy(fscr[:, 1:2], qtot[:, (j + 1) * 1024 - 1:(j + 1) * 1024])
            tPrevB = q.on("vector", ropeq, after=[tEvQr, tLD])

            def vmm(e, j=j):
                for n in range(2):
                    for lt in range(4):
                        ins = e.matmul(psV[:, n * 512:(n + 1) * 512],
                                       kvdT[:, lt * S + j * 128:lt * S + (j + 1) * 128],
                                       vub[:, lt * 1024 + n * 512:lt * 1024 + (n + 1) * 512],
                                       start=(lt == 0), stop=(lt == 3))
                return ins
            tV = q.on("tensor", vmm, after=[tEvPrev, tPrevV])

            def vev(e, j=j):
                e.tensor_copy(vtot[:, (j % 2) * 1024:(j % 2 + 1) * 1024], psV[:])
                return e.tensor_copy(fscr[:, 4:5], vtot[:, (j % 2) * 1024 + 1023:(j % 2) * 1024 + 1024])
            tPrevV = q.on("vector", vev, after=[tV, tVst[j % 2]])

            def stv(e, sem, j=j):
                e.dma_start(out=vt_d[128 * j:128 * (j + 1), :],
                            in_=vtot[:, (j % 2) * 1024:(j % 2 + 1) * 1024]).then_inc(sem, 16)
            tVst[j % 2] = q.on("sync", stv, after=[tPrevV], ndma=1)

        tTr = None
        tQst = None
        for src_t, dst_d, tok0 in ((qtot, qT_d, tPrevB), (ktot, kT_d, tPrevA)):
            for a in range(8):
                for half in range(2):
                    def trq(e, a=a, half=half, src_t=src_t):
                        for jj in range(8):
                            j = 8 * half + jj
                            ins = e.transpose(psT[:, jj * 128:(jj + 1) * 128],
                                              src_t[:, j * 1024 + a * 128:j * 1024 + (a + 1) * 128], idt[:])
                        return ins
                    tt = q.on("tensor", trq, after=[tok0, tRopeK, tTr])

                    def evq(e, half=half):
                        for jj in range(8):
                            ins = e.copy(orow[:, half * 1024 + jj * 128:half * 1024 + (jj + 1) * 128],
                                         psT[:, jj * 128:(jj + 1) * 128])
                        return e.copy(fscr[:, 2:3], orow[:, half * 1024 + 1023:half * 1024 + 1024])
                    tTr = q.on("scalar", evq, after=[tt, tQst if half == 0 else None])

                def stq(e, sem, a=a, dst_d=dst_d):
                    e.dma_start(out=dst_d[a * 128:(a + 1) * 128, :], in_=orow[:]).then_inc(sem, 16)
                tQst = q.on("sync", stq, after=[tTr], ndma=1)

        q.emit(block)
        q.finish(block, [tQst, tVst[0], tVst[1]])
    return nc


def host_mla_proj(inputs, xn1_parts):
    xn1 = np.concatenate([np.asarray(p) for p in xn1_parts], 0).reshape(B, S, H)
    kvd = np.asarray(inputs["L1_ffn_kv_d_w"], np.float32)
    qd = np.asarray(inputs["L1_ffn_q_d_w"], np.float32)
    ku = np.asarray(inputs["L1_ffn_k_u_w"], np.float32)
    qu = np.asarray(inputs["L1_ffn_q_u_w"], np.float32)
    vu = np.asarray(inputs["L1_ffn_v_u_w"], np.float32)
    rk = np.asarray(inputs["L1_ffn_rope_k_w"], np.float32)
    rq = np.asarray(inputs["L1_ffn_rope_q_w"], np.float32)
    cosm, sinm = _mla_costab()
    cost, sint = _tiles128(cosm), _tiles128(sinm)
    in_maps = []
    for c in range(NCORES):
        b, hh = c // 2, c % 2
        cs = slice(512 * hh, 512 * (hh + 1))
        in_maps.append({
            "xT": np.ascontiguousarray(xn1[b].T),
            "kvd": kvd, "qd": qd,
            "ku": np.ascontiguousarray(ku[:, cs]),
            "qu": np.ascontiguousarray(qu[:, cs]),
            "vu": np.ascontiguousarray(vu[:, 1024 * hh:1024 * (hh + 1)]),
            "rk": np.ascontiguousarray(rk[:, cs]),
            "rq": np.ascontiguousarray(rq[:, cs]),
            "cosm": cost, "sinm": sint,
            "ident": IDENT,
        })
    return in_maps



def build_moe(dbg_w=False):
    nc = bass.Bass("TRN2", target_bir_lowering=False, debug=False)
    xT_d = nc.dram_tensor("xT", [H, TT_], BF16, kind="ExternalInput").ap()
    wg_d = nc.dram_tensor("wg", [H, 1024], F32, kind="ExternalInput").ap()
    wu_d = nc.dram_tensor("wu", [H, 1024], F32, kind="ExternalInput").ap()
    wd_d = nc.dram_tensor("wd", [1024, H], F32, kind="ExternalInput").ap()
    rw_d = nc.dram_tensor("rw", [H, 7], F32, kind="ExternalInput").ap()
    rb_d = nc.dram_tensor("rb", [1, 7], F32, kind="ExternalInput").ap()
    ws_d = nc.dram_tensor("wsel", [1, 2], F32, kind="ExternalInput").ap()
    out_d = nc.dram_tensor("partial", [TT_, H], BF16, kind="ExternalOutput").ap()
    w_d = nc.dram_tensor("wdbg", [128, 64], F32, kind="ExternalOutput").ap() if dbg_w else None
    pr_d = nc.dram_tensor("prdbg", [128, 448], F32, kind="ExternalOutput").ap() if dbg_w else None

    with ExitStack() as es:
        sb = lambda n, sh, dt: es.enter_context(nc.sbuf_tensor(n, sh, dt))
        xcb = sb("xcb", [128, 16 * 512], BF16)
        wgb = sb("wgb", [128, 16 * 1024], BF16)
        wub = sb("wub", [128, 16 * 1024], BF16)
        wdb = sb("wdb", [128, 8 * H], BF16)
        rwb = sb("rwb", [128, 16 * 7], BF16)
        rbb = sb("rbb", [128, 7], F32)
        wselc = sb("wselc", [128, 2], F32)
        hT = sb("hT", [128, 8 * 512], BF16)
        sgf = sb("sgf", [128, 512], F32)
        t1f = sb("t1f", [128, 512], F32)
        lgAll = sb("lgAll", [128, 448], F32)
        prAll = sb("prAll", [128, 448], F32)
        m1rep = sb("m1rep", [128, 448], F32)
        eqw = sb("eqw", [128, 448], F32)
        pmAll = sb("pmAll", [128, 448], F32)
        m1 = sb("m1", [128, 64], F32)
        m2 = sb("m2", [128, 64], F32)
        dc = sb("dc", [128, 64], F32)
        sgc = sb("sgc", [128, 64], F32)
        u1 = sb("u1", [128, 64], F32)
        nn = sb("nn", [128, 64], F32)
        s12 = sb("s12", [128, 64], F32)
        invc = sb("invc", [128, 64], F32)
        wAll = sb("wAll", [128, 64], F32)
        dscr = sb("dscr", [128, 512], F32)
        obuf = sb("obuf", [128, 2 * H], BF16)
        fscr = sb("fscr", [128, 64], F32)
        psR = es.enter_context(nc.psum_tensor("psR", [128, 512], F32))
        psG = es.enter_context(nc.psum_tensor("psG", [128, 512], F32))
        psU = es.enter_context(nc.psum_tensor("psU", [128, 512], F32))
        psD = es.enter_context(nc.psum_tensor("psD", [128, 2048], F32))
        block = es.enter_context(nc.Block())
        q = Seq(es, nc)

        def ld(e, sem):
            e.dma_start(out=rbb[:], in_=rb_d.broadcast_to([128, 7])).then_inc(sem, 16)
            e.dma_start(out=wselc[:], in_=ws_d.broadcast_to([128, 2])).then_inc(sem, 16)
        tLD = q.on("sync", ld, ndma=2)

        def ldw(e, sem):
            for i in range(16):
                e.dma_start(out=rwb[:, i * 7:(i + 1) * 7],
                            in_=rw_d[128 * i:128 * (i + 1), :]).then_inc(sem, 16)
            for i in range(16):
                e.dma_start(out=wgb[:, i * 1024:(i + 1) * 1024],
                            in_=wg_d[128 * i:128 * (i + 1), :]).then_inc(sem, 16)
            for i in range(16):
                e.dma_start(out=wub[:, i * 1024:(i + 1) * 1024],
                            in_=wu_d[128 * i:128 * (i + 1), :]).then_inc(sem, 16)
            for i in range(8):
                e.dma_start(out=wdb[:, i * H:(i + 1) * H],
                            in_=wd_d[128 * i:128 * (i + 1), :]).then_inc(sem, 16)
        tW = q.on("gpsimd", ldw, ndma=56)

        tXc = None
        tREv = None
        for tc in range(16):
            def ldxc(e, sem, tc=tc):
                for i in range(16):
                    e.dma_start(out=xcb[:, i * 512:(i + 1) * 512],
                                in_=xT_d[128 * i:128 * (i + 1), 512 * tc:512 * (tc + 1)]).then_inc(sem, 16)
            tXc = q.on("sync", ldxc, after=[tREv], ndma=16)

            for u in range(4):
                tt = tc * 4 + u

                def rmm(e, u=u):
                    for i in range(16):
                        ins = e.matmul(psR[:, (u % 2) * 128:(u % 2) * 128 + 7],
                                       xcb[:, i * 512 + u * 128:i * 512 + (u + 1) * 128],
                                       rwb[:, i * 7:(i + 1) * 7],
                                       start=(i == 0), stop=(i == 15))
                    return ins
                tRm = q.on("tensor", rmm, after=[tXc, tW, tREv])

                def rev(e, tt=tt, u=u):
                    e.scalar_tensor_tensor(
                        lgAll[:].rearrange("p (s g) -> p s g", s=7)[:, :, tt:tt + 1],
                        psR[:, (u % 2) * 128:(u % 2) * 128 + 7], 1.0, rbb[:],
                        ALU.mult, ALU.add)
                    return e.tensor_copy(fscr[:, 0:1], lgAll[:, 384 + tt:385 + tt])
                tREv = q.on("vector", rev, after=[tRm])

        def sprob(e):
            e.activation(prAll[:], lgAll[:], AF.Sigmoid)
            return e.copy(fscr[:, 0:32], prAll[:, 416:448])
        tS0 = q.on("scalar", sprob, after=[tREv])

        def vm1(e):
            e.reduce_max(m1[:], prAll[:].rearrange("p (s g) -> p g s", g=64), axis=AX)
            return e.tensor_copy(fscr[:, 0:64], m1[:])
        tM1 = q.on("vector", vm1, after=[tS0])

        def repm1(e, sem):
            e.dma_start(out=m1rep[:].rearrange("p (s g) -> p s g", s=7),
                        in_=m1[:].unsqueeze(1).broadcast_to([128, 7, 64])).then_inc(sem, 16)
        tRep = q.on("gpsimd", repm1, after=[tM1], ndma=1)

        def vm2(e):
            e.tensor_tensor(eqw[:], prAll[:], m1rep[:], ALU.is_equal)
            e.scalar_tensor_tensor(pmAll[:], eqw[:], -1e30, prAll[:], ALU.mult, ALU.add)
            e.reduce_max(m2[:], pmAll[:].rearrange("p (s g) -> p g s", g=64), axis=AX)
            return e.tensor_copy(fscr[:, 0:64], m2[:])
        tM2 = q.on("vector", vm2, after=[tRep])

        def sm2(e):
            e.copy(dscr[:, 0:64], m2[:])
            return e.copy(fscr[:, 0:64], dscr[:, 0:64])
        tM2h = q.on("scalar", sm2, after=[tM2])

        def vdc(e):
            e.scalar_tensor_tensor(dc[:], dscr[:, 0:64], -0.9999999, prAll[:, 0:64],
                                   ALU.mult, ALU.add)
            e.tensor_tensor(s12[:], m1[:], dscr[:, 0:64], ALU.add)
            return e.tensor_copy(fscr[:, 0:64], dc[:])
        tDc = q.on("vector", vdc, after=[tM2h])

        def ssg(e):
            e.activation(sgc[:], dc[:], AF.Sign)
            return e.copy(fscr[:, 0:64], sgc[:])
        tSg = q.on("scalar", ssg, after=[tDc])

        def vwf(e):
            e.reciprocal(invc[:], s12[:])
            e.tensor_tensor(u1[:], sgc[:], prAll[:, 0:64], ALU.mult)
            e.tensor_copy(dscr[:], dscr[:])
            e.tensor_tensor(nn[:], u1[:], prAll[:, 0:64], ALU.add)
            e.tensor_copy(dscr[:], dscr[:])
            e.tensor_tensor(nn[:], nn[:], invc[:], ALU.mult)
            e.tensor_copy(dscr[:], dscr[:])
            e.tensor_scalar(wAll[:], nn[:], wselc[:, 1:2], wselc[:, 0:1], ALU.mult, ALU.add)
            return e.tensor_copy(fscr[:, 0:64], wAll[:])
        tWfin = q.on("vector", vwf, after=[tSg])

        tDevPrev = [None] * 2
        tDevLast = None
        tOst = [None] * 2
        tGU = None
        tHt = None
        for tc in range(16):
            def ldxc2(e, sem, tc=tc):
                for i in range(16):
                    e.dma_start(out=xcb[:, i * 512:(i + 1) * 512],
                                in_=xT_d[128 * i:128 * (i + 1), 512 * tc:512 * (tc + 1)]).then_inc(sem, 16)
            tXc = q.on("sync", ldxc2, after=[tGU, tWfin if tc == 0 else None], ndma=16)

            for it in range(8):
                def gumm(e, it=it):
                    for i in range(16):
                        e.matmul(psG[:, 0:512],
                                 wgb[:, i * 1024 + it * 128:i * 1024 + (it + 1) * 128],
                                 xcb[:, i * 512:(i + 1) * 512],
                                 start=(i == 0), stop=(i == 15))
                    for i in range(16):
                        ins = e.matmul(psU[:, 0:512],
                                       wub[:, i * 1024 + it * 128:i * 1024 + (it + 1) * 128],
                                       xcb[:, i * 512:(i + 1) * 512],
                                       start=(i == 0), stop=(i == 15))
                    return ins
                tGU = q.on("tensor", gumm, after=[tXc, tW, tHt])

                def ssig(e):
                    e.activation(sgf[:], psG[:, 0:512], AF.Sigmoid)
                    return e.copy(fscr[:, 0:1], sgf[:, 511:512])
                tSgs = q.on("scalar", ssig, after=[tGU])

                def vht(e, it=it):
                    e.tensor_tensor(t1f[:], sgf[:], psG[:, 0:512], ALU.mult)
                    e.scalar_tensor_tensor(hT[:, it * 512:(it + 1) * 512], t1f[:], 0.125,
                                           psU[:, 0:512], ALU.mult, ALU.mult)
                    return e.tensor_copy(fscr[:, 0:1], hT[:, (it + 1) * 512 - 1:(it + 1) * 512])
                tHt = q.on("vector", vht, after=[tSgs])

            for u in range(4):
                slot = u % 2
                tt = tc * 4 + u

                def dmm(e, u=u):
                    for n in range(4):
                        for i in range(8):
                            ins = e.matmul(psD[:, n * 512:(n + 1) * 512],
                                           hT[:, i * 512 + u * 128:i * 512 + (u + 1) * 128],
                                           wdb[:, i * H + n * 512:i * H + (n + 1) * 512],
                                           start=(i == 0), stop=(i == 7))
                    return ins
                tDm = q.on("tensor", dmm, after=[tHt, tDevLast])

                def dev(e, tt=tt, slot=slot):
                    e.tensor_scalar(obuf[:, slot * H:(slot + 1) * H], psD[:], wAll[:, tt:tt + 1],
                                    None, ALU.mult)
                    return e.tensor_copy(fscr[:, 0:1], obuf[:, slot * H + H - 1:slot * H + H])
                tDevPrev[slot] = q.on("vector", dev, after=[tDm, tWfin, tOst[slot]])
                tDevLast = tDevPrev[slot]

                def ost(e, sem, tc=tc, u=u, slot=slot):
                    e.dma_start(out=out_d[tc * 512 + u * 128:tc * 512 + (u + 1) * 128, :],
                                in_=obuf[:, slot * H:(slot + 1) * H]).then_inc(sem, 16)
                tOst[slot] = q.on("sync", ost, after=[tDevPrev[slot]], ndma=1)

        if dbg_w:
            def dmpw(e, sem):
                e.dma_start(out=w_d, in_=wAll[:]).then_inc(sem, 16)
                e.dma_start(out=pr_d, in_=prAll[:]).then_inc(sem, 16)
            tDW = q.on("sync", dmpw, after=[tWfin], ndma=2)
            q.emit(block)
            q.finish(block, [tOst[0], tOst[1], tDW])
        else:
            q.emit(block)
            q.finish(block, [tOst[0], tOst[1]])
    return nc


def host_moe(inputs, xn2_parts):
    xn2 = np.concatenate([np.asarray(p) for p in xn2_parts], 0)
    xT = np.ascontiguousarray(xn2.T)
    shg = np.asarray(inputs["L1_ffn_sh_gate"], np.float32)
    shu = np.asarray(inputs["L1_ffn_sh_up"], np.float32)
    shd = np.asarray(inputs["L1_ffn_sh_down"], np.float32)
    rg = np.asarray(inputs["L1_ffn_r_gate"], np.float32)
    ru = np.asarray(inputs["L1_ffn_r_up"], np.float32)
    rd = np.asarray(inputs["L1_ffn_r_down"], np.float32)
    rw = np.asarray(inputs["L1_ffn_router_w"], np.float32)
    rb = np.asarray(inputs["L1_ffn_router_b"], np.float32)
    in_maps = []
    for c in range(NCORES):
        if c == 0:
            wg, wu, wd = shg, shu, shd
            perm = list(range(7))
            wsel = np.array([[1.0, 0.0]], np.float32)
        else:
            wg, wu, wd = rg[c - 1], ru[c - 1], rd[c - 1]
            perm = [c - 1] + [i for i in range(7) if i != c - 1]
            wsel = np.array([[0.0, 0.5]], np.float32)
        in_maps.append({
            "xT": xT,
            "wg": np.ascontiguousarray(wg), "wu": np.ascontiguousarray(wu),
            "wd": np.ascontiguousarray(wd),
            "rw": np.ascontiguousarray(rw[:, perm]),
            "rb": np.ascontiguousarray(rb[perm].reshape(1, 7)),
            "wsel": wsel,
        })
    return in_maps



_L1_COLMAPS = [np.arange(1024, dtype=np.int64), np.arange(1024, 2048, dtype=np.int64)]


def _res(r):
    return [{k: np.ascontiguousarray(np.asarray(v)) for k, v in rr.items()} for rr in r.results]


def kernel(**inputs):
    nc = build_l0_qkv()
    a1r = _res(_run(nc, host_l0_qkv(inputs)))
    nc = build_l0_attn()
    a2r = _res(_run(nc, host_l0_attn(inputs, a1r)))
    nc = build_oproj_ln()
    r = _res(_run(nc, host_oproj_ln(inputs, a2r)))
    x1f = [x["x1f"] for x in r]
    x1b = [x["x1b"] for x in r]
    nc = build_l0_ffn()
    rC = _res(_run(nc, host_l0_ffn(inputs, x1b)))
    nc = build_reduce_ln()
    r = _res(_run(nc, host_reduce_ln([x["partial"] for x in rC], x1f)))
    x2f = [x["yf"] for x in r]
    x2b = [x["yb"] for x in r]
    nc = build_l1_qkv()
    e1r = _res(_run(nc, host_l1_qkv(inputs, x2b)))
    nc = build_causal_attn(256, 4)
    e2r = _res(_run(nc, host_causal_attn(e1r)))
    woT = np.ascontiguousarray(np.asarray(inputs["L1_attn_Wo"], np.float32).T)
    nc = build_oproj_block(True)
    r = _res(_run(nc, host_oproj_block([x["ytok"] for x in e2r], _L1_COLMAPS, woT,
                                       x2f, np.asarray(inputs["L1_ffn_ln1_w"], np.float32))))
    x3f = [x["yf"] for x in r]
    xn1 = [x["rn"] for x in r]
    nc = build_mla_proj()
    g1r = _res(_run(nc, host_mla_proj(inputs, xn1)))
    nc = build_causal_attn(1024, 1)
    g2r = _res(_run(nc, host_causal_attn(g1r)))
    ow = np.ascontiguousarray(np.asarray(inputs["L1_ffn_o_w"], np.float32))
    nc = build_oproj_block(False)
    r = _res(_run(nc, host_oproj_block([x["ytok"] for x in g2r], _L1_COLMAPS, ow,
                                       x3f, np.asarray(inputs["L1_ffn_ln2_w"], np.float32))))
    hf = [x["yf"] for x in r]
    xn2 = [x["rn"] for x in r]
    nc = build_moe()
    rI = _res(_run(nc, host_moe(inputs, xn2)))
    nc = build_reduce_ln(npart=8, emit_bf16=False, nresid=2)
    r = _res(_run(nc, host_reduce_ln([x["partial"] for x in rI], hf, token_major=True,
                                     x2_parts=x3f)))
    out = np.concatenate([x["yf"] for x in r], axis=0).reshape(B, S, H)
    return out.astype(np.float32)
